# version 08
# baseline (speedup 1.0000x reference)
import numpy as np
from contextlib import ExitStack

import concourse.bass as bass
import concourse.mybir as mybir
from concourse.bass_utils import run_bass_kernel_spmd

F32 = mybir.dt.float32
BF16 = mybir.dt.bfloat16
I32 = mybir.dt.int32
AF = mybir.ActivationFunctionType
ALU = mybir.AluOpType
AX = mybir.AxisListType

E = 2048
S = 2048
HID = 5632
H = 16
DH = 128
ECH = E // 128
HCH = HID // 128
TF = 512
NTT = (2 * S) // TF
NCORES = 8

ENGINES = ("sync", "gpsimd", "vector", "scalar", "tensor")
ENG_SEM = {"tensor": "P", "vector": "V", "scalar": "A", "gpsimd": "g"}


class Plan:

    def __init__(self, nc):
        self.nc = nc
        self.es = ExitStack()
        self.ops = {e: [] for e in ENGINES}
        self.cnt = {}
        self.sems = {}
        self.pending = {e: [] for e in ENGINES}
        for name in ("P", "V", "A", "g"):
            self.sems[name] = self.es.enter_context(nc.semaphore(name))
            self.cnt[name] = 0

    def _sem(self, name):
        if name not in self.sems:
            self.sems[name] = self.es.enter_context(self.nc.semaphore(name))
            self.cnt[name] = 0
        return name

    def o(self, eng, fn, w=(), inc=None, n=1):
        waits = [x for x in self.pending[eng] if x is not None] + \
                [x for x in w if x is not None and x[1]]
        self.pending[eng] = []
        rec = [fn, waits, inc, n]
        self.ops[eng].append(rec)
        if inc is not None:
            self.cnt[inc] += n
            return self.cnt[inc]
        return None

    def dma(self, eng, out, in_, w=(), slot="misc"):
        sem = self._sem("D_" + slot)
        self.o(eng, lambda e, o=out, i=in_: e.dma_start(out=o, in_=i),
               w=w, inc=sem, n=16)
        return (sem, self.cnt[sem])

    def barrier(self):
        for e in ENGINES:
            sem = ENG_SEM.get(e)
            if sem and self.ops[e]:
                rec = self.ops[e][-1]
                if rec[2] is None:
                    rec[2] = sem
                    rec[3] = 1
                    self.cnt[sem] += 1
        waits = [(s, v) for s, v in self.cnt.items() if v > 0]
        for e in ENGINES:
            self.pending[e] = list(waits)

    def emit(self):
        with self.nc.Block() as blk:
            for e in ENGINES:
                ops = self.ops[e]
                sems = self.sems

                def body(eng, ops=ops, sems=sems):
                    hw = {}
                    for fn, waits, inc, n in ops:
                        for s, v in waits:
                            if hw.get(s, 0) < v:
                                eng.wait_ge(sems[s], v)
                                hw[s] = v
                        inst = fn(eng)
                        if inc is not None:
                            inst.then_inc(sems[inc], n)
                getattr(blk, e)(body)


def build_program(phases=("A", "B", "C"), debug=False):
    nc = bass.Bass("TRN2", target_bir_lowering=False)

    d_xT = nc.dram_tensor("xT", [2, E, S], F32, kind="ExternalInput")
    d_gblk = nc.dram_tensor("gblk", [HCH, ECH, 128, 128], I32, kind="ExternalInput")
    d_ublk = nc.dram_tensor("ublk", [HCH, ECH, 128, 128], I32, kind="ExternalInput")
    d_dblk = nc.dram_tensor("dblk", [ECH, HCH, 128, 128], I32, kind="ExternalInput")
    d_qblk = nc.dram_tensor("qblk", [H, ECH, 128, 128], F32, kind="ExternalInput")
    d_kblk = nc.dram_tensor("kblk", [H, ECH, 128, 128], F32, kind="ExternalInput")
    d_vblk = nc.dram_tensor("vblk", [H // 4, ECH, 4, 128, 128], F32, kind="ExternalInput")
    d_oblk = nc.dram_tensor("oblk", [H, ECH, 128, 128], F32, kind="ExternalInput")
    d_sg = nc.dram_tensor("sg", [128, HCH], F32, kind="ExternalInput")
    d_su = nc.dram_tensor("su", [128, HCH], F32, kind="ExternalInput")
    d_sd = nc.dram_tensor("sd", [128, ECH], F32, kind="ExternalInput")
    d_qs = nc.dram_tensor("qs", [128, 1], F32, kind="ExternalInput")
    d_ks = nc.dram_tensor("ks", [128, 1], F32, kind="ExternalInput")
    d_yT = nc.dram_tensor("yT", [2, E, S], F32, kind="ExternalOutput")

    d_gsc = nc.dram_tensor("gsc", [HCH, 128, E], BF16, kind="Internal")
    d_usc = nc.dram_tensor("usc", [HCH, 128, E], BF16, kind="Internal")
    d_dsc = nc.dram_tensor("dsc", [ECH, 128, HID], BF16, kind="Internal")
    d_osc = nc.dram_tensor("osc", [H, 128, E], BF16, kind="Internal")
    d_aosc = nc.dram_tensor("aosc", [H, DH, S], BF16, kind="Internal")
    d_x1 = nc.dram_tensor("x1sc", [2, E, S], F32, kind="ExternalOutput" if debug else "Internal")
    d_x2 = nc.dram_tensor("x2sc", [2, E, S], F32, kind="ExternalOutput" if debug else "Internal")
    d_dbg = nc.dram_tensor("dbg", [128, 4096], F32,
                           kind="ExternalOutput" if debug else "Internal")

    p = Plan(nc)
    es_glob = ExitStack()
    sb = lambda name, shape, dt: es_glob.enter_context(nc.sbuf_tensor(name, shape, dt))

    ones1 = sb("ones1", [128, 1], BF16)
    onesrow = sb("onesrow", [1, 128], BF16)
    onesrowF = sb("onesrowF", [1, 128], F32)
    ones2 = sb("ones2", [2, 128], BF16)
    iot_r = sb("iot_r", [128, 128], F32)
    iot_c = sb("iot_c", [128, 1], F32)
    idf = sb("idf", [128, 128], F32)
    sgc = sb("sgc", [128, HCH], F32)
    suh = sb("suh", [128, HCH], F32)
    sdc = sb("sdc", [128, ECH], F32)
    qsc = sb("qsc", [128, 1], F32)
    ksc = sb("ksc", [128, 1], F32)
    eps5 = sb("eps5", [128, 1], F32)
    eps6 = sb("eps6", [128, 1], F32)

    PSA = es_glob.enter_context(nc.psum_tensor("PSA", [128, 2048], F32))
    PSB = es_glob.enter_context(nc.psum_tensor("PSB", [128, 2048], F32))
    PS = (PSA, PSB)

    p.dma("sync", sgc[:], d_sg.ap(), slot="setup")
    p.dma("sync", suh[:], d_su.ap(), slot="setup")
    p.dma("sync", sdc[:], d_sd.ap(), slot="setup")
    p.dma("sync", qsc[:], d_qs.ap(), slot="setup")
    lS = p.dma("sync", ksc[:], d_ks.ap(), slot="setup")
    p.o("gpsimd", lambda e: e.iota(iot_r[:], pattern=[[1, 128]], base=0,
                                   channel_multiplier=0, allow_small_or_imprecise_dtypes=True))
    p.o("gpsimd", lambda e: e.iota(iot_c[:], pattern=[[0, 1]], base=0,
                                   channel_multiplier=1, allow_small_or_imprecise_dtypes=True))
    p.o("gpsimd", lambda e: e.memset(ones1[:], 1.0))
    p.o("gpsimd", lambda e: e.memset(onesrow[:], 1.0))
    p.o("gpsimd", lambda e: e.memset(onesrowF[:], 1.0))
    p.o("gpsimd", lambda e: e.memset(eps5[:], 1e-5))
    p.o("gpsimd", lambda e: e.memset(eps6[:], 1e-6))
    gv = p.o("gpsimd", lambda e: e.memset(ones2[:], 1.0), inc="g")
    p.o("vector", lambda e: e.tensor_scalar(out=idf[:], in0=iot_r[:], scalar1=iot_c[:],
                                            scalar2=None, op0=ALU.is_equal), w=[("g", gv)])
    p.o("vector", lambda e: e.tensor_scalar(out=suh[:], in0=suh[:], scalar1=0.5,
                                            scalar2=None, op0=ALU.mult),
        w=[lS], inc="V")
    p.barrier()

    def plan_ffn(src, dst, build_scratch, tag=""):
        es = ExitStack()
        s_ = lambda name, shape, dt: es.enter_context(nc.sbuf_tensor(name + tag, shape, dt))
        xca = s_("xca", [128, 2 * ECH * TF], BF16)
        hsl = s_("hsl", [128, HCH * TF], BF16)
        wg = s_("wg", [128, 2 * E], BF16)
        wu = s_("wu", [128, 2 * E], BF16)
        wd = s_("wd", [128, 2 * HID], BF16)
        stg = s_("stg", [128, 2 * 2048], I32)
        xf = s_("xf", [128, 2 * TF], F32)
        sqb = s_("sqb", [128, 2 * TF], BF16)
        row_m = s_("row_m", [1, TF], F32)
        row_v = s_("row_v", [1, TF], F32)
        row_r = s_("row_r", [1, TF], F32)
        row_mbf = s_("row_mbf", [1, TF], BF16)
        row_rbf = s_("row_rbf", [1, TF], BF16)
        mbc = s_("mbc", [128, 2 * TF], BF16)
        rbc = s_("rbc", [128, 2 * TF], BF16)
        g1b = s_("g1b", [128, 2 * TF], F32)
        gsb = s_("gsb", [128, 2 * TF], BF16)
        ub = s_("ub", [128, 2 * TF], BF16)
        xep = s_("xep", [128, 2 * TF], F32)
        yst = s_("yst", [128, 2 * TF], F32)

        war = {"wg": [0, 0], "wu": [0, 0], "wd": [0, 0], "xf": [0, 0], "sqb": [0, 0],
               "pgpu": [0, 0], "pd": [0, 0], "yst_st": [None, None], "xep": [0, 0],
               "xca_pe": [0, 0], "stgV": 0, "stgV2": [0, 0], "stgn": 0,
               "rowsP": 0, "rowsA": 0,
               "wgst": [None, None], "wust": [None, None], "wdst": [None, None]}
        vh_last = [0] * NTT
        vxc = [0] * NTT
        dsc_stored = [None] * ECH
        usc_stored = [None] * HCH

        def plan_stats(tt):
            s, c0, par = tt // 4, (tt % 4) * TF, tt % 2
            xo = par * ECH * TF
            pcs = PSA[0:1, 1024:1536]
            psq = PSB[0:1, 1024:1536]
            lastP = None
            for dc in range(ECH):
                dp = dc % 2
                xfv = xf[:, dp * TF:(dp + 1) * TF]
                lx = p.dma("sync", xfv, src.ap()[s, dc * 128:(dc + 1) * 128, c0:c0 + TF],
                           w=[("P", war["xf"][dp])], slot="xf%d" % dp)
                xcv = xca[:, xo + dc * TF: xo + (dc + 1) * TF]
                vc = p.o("vector", lambda e, o=xcv, i=xfv: e.tensor_copy(out=o, in_=i),
                         w=[lx, ("P", war["xca_pe"][par]) if dc == 0 else None],
                         inc="V")
                sqv = sqb[:, dp * TF:(dp + 1) * TF]
                vs = p.o("vector", lambda e, o=sqv, i=xfv:
                         e.tensor_tensor(out=o, in0=i, in1=i, op=ALU.mult),
                         w=[("P", war["sqb"][dp])], inc="V")
                p.o("tensor", lambda e, o=pcs, r=xcv, st=(dc == 0), sp=(dc == ECH - 1):
                    e.matmul(out=o, lhsT=ones1[:], rhs=r, start=st, stop=sp),
                    w=[("V", vc)])
                lastP = p.o("tensor", lambda e, o=psq, r=sqv, st=(dc == 0), sp=(dc == ECH - 1):
                            e.matmul(out=o, lhsT=ones1[:], rhs=r, start=st, stop=sp),
                            w=[("V", vs)], inc="P")
                war["xf"][dp] = lastP
                war["sqb"][dp] = lastP
            v1 = p.o("vector", lambda e: e.tensor_scalar(out=row_m[:], in0=pcs, scalar1=1.0 / E,
                                                         scalar2=None, op0=ALU.mult),
                     w=[("P", lastP), ("P", war["rowsP"]), ("A", war["rowsA"])], inc="V")
            v1b = p.o("vector", lambda e: e.tensor_scalar(out=row_v[:], in0=psq, scalar1=1.0 / E,
                                                          scalar2=None, op0=ALU.mult),
                      w=[("V", v1)], inc="V")
            v1c = p.o("vector", lambda e: e.tensor_tensor(out=row_r[:], in0=row_m[:], in1=row_m[:],
                                                          op=ALU.mult), w=[("V", v1b)], inc="V")
            v2 = p.o("vector", lambda e: e.tensor_tensor(out=row_v[:], in0=row_v[:], in1=row_r[:],
                                                         op=ALU.subtract), w=[("V", v1c)], inc="V")
            a0 = p.o("scalar", lambda e: e.activation(out=row_r[:], in_=row_v[:], func=AF.Ln,
                                                      bias=eps5[0:1, :], scale=1.0),
                     w=[("V", v2)], inc="A")
            a1 = p.o("scalar", lambda e: e.activation(out=row_r[:], in_=row_r[:], func=AF.Exp,
                                                      scale=-0.5), w=[("A", a0)], inc="A")
            v3 = p.o("vector", lambda e: e.tensor_copy(out=row_mbf[:], in_=row_m[:]),
                     w=[("V", v2)], inc="V")
            pb1 = p.o("tensor", lambda e: e.matmul(out=PSA[:, 1024:1536], lhsT=onesrow[:],
                                                   rhs=row_mbf[:], start=True, stop=True),
                      w=[("V", v3)], inc="P")
            v5 = p.o("vector", lambda e, o=mbc[:, par * TF:(par + 1) * TF]:
                     e.tensor_copy(out=o, in_=PSA[:, 1024:1536]), w=[("P", pb1)], inc="V")
            v4 = p.o("vector", lambda e: e.tensor_copy(out=row_rbf[:], in_=row_r[:]),
                     w=[("A", a1)], inc="V")
            pb2 = p.o("tensor", lambda e: e.matmul(out=PSB[:, 1024:1536], lhsT=onesrow[:],
                                                   rhs=row_rbf[:], start=True, stop=True),
                      w=[("V", v4)], inc="P")
            p.o("vector", lambda e, o=rbc[:, par * TF:(par + 1) * TF]:
                e.tensor_copy(out=o, in_=PSB[:, 1024:1536]), w=[("P", pb2)], inc="V")
            war["rowsP"] = pb2
            war["rowsA"] = a1
            last = None
            for dc in range(ECH):
                last = p.o("vector", lambda e, o=xca[:, xo + dc * TF: xo + (dc + 1) * TF],
                           m=mbc[:, par * TF:(par + 1) * TF]:
                           e.tensor_tensor(out=o, in0=o, in1=m, op=ALU.subtract),
                           w=[("V", v5)] if dc == 0 else (),
                           inc="V" if dc == ECH - 1 else None)
            vxc[tt] = last

        def plan_gateup(tt):
            s, c0, par = tt // 4, (tt % 4) * TF, tt % 2
            xo = par * ECH * TF
            lastV = 0
            for hc in range(HCH):
                wp = hc % 2
                if build_scratch and tt == 0:
                    for (dram, wbuf, scn, wark, warst, stored) in (
                            (d_gblk, wg, d_gsc, "wg", "wgst", None),
                            (d_ublk, wu, d_usc, "wu", "wust", usc_stored)):
                        sp_ = war["stgn"] % 2
                        war["stgn"] += 1
                        stgh = stg[:, sp_ * 2048:(sp_ + 1) * 2048]
                        lw = p.dma("sync", stgh.rearrange("p (g n) -> p g n", g=ECH),
                                   dram.ap()[hc].rearrange("g p n -> p g n"),
                                   w=[("V", war["stgV2"][sp_])], slot="stg%d" % sp_)
                        vd = p.o("vector", lambda e, o=wbuf[:, wp * E:(wp + 1) * E], i=stgh:
                                 e.tensor_scalar(out=o, in0=i, scalar1=-128.0,
                                                 scalar2=None, op0=ALU.add),
                                 w=[lw, ("P", war[wark][wp]), war[warst][wp]],
                                 inc="V")
                        war["stgV2"][sp_] = vd
                        war["stgV"] = vd
                        gst = p.dma("gpsimd", scn.ap()[hc], wbuf[:, wp * E:(wp + 1) * E],
                                    w=[("V", vd)],
                                    slot=("gst%d" if scn is d_gsc else "ust%d") % wp)
                        war[warst][wp] = gst
                        if stored is not None:
                            stored[hc] = gst
                    lwv = [("V", war["stgV"])]
                else:
                    lg = p.dma("sync", wg[:, wp * E:(wp + 1) * E], d_gsc.ap()[hc],
                               w=[("P", war["wg"][wp]),
                                  usc_stored[hc] if build_scratch else None],
                               slot="wg%d" % wp)
                    lu = p.dma("sync", wu[:, wp * E:(wp + 1) * E], d_usc.ap()[hc],
                               w=[("P", war["wu"][wp])], slot="wu%d" % wp)
                    lwv = [lg, lu]
                pg = PS[wp][:, 0:512]
                pu = PS[wp][:, 512:1024]
                firstw = lwv + [("V", vxc[tt]), ("V", war["pgpu"][wp])]
                for dc in range(ECH):
                    p.o("tensor", lambda e, o=pg,
                        l=wg[:, wp * E + dc * 128: wp * E + (dc + 1) * 128],
                        r=xca[:, xo + dc * TF: xo + (dc + 1) * TF],
                        st=(dc == 0), sp=(dc == ECH - 1):
                        e.matmul(out=o, lhsT=l, rhs=r, start=st, stop=sp),
                        w=firstw if dc == 0 else ())
                pGU = None
                for dc in range(ECH):
                    pGU = p.o("tensor", lambda e, o=pu,
                              l=wu[:, wp * E + dc * 128: wp * E + (dc + 1) * 128],
                              r=xca[:, xo + dc * TF: xo + (dc + 1) * TF],
                              st=(dc == 0), sp=(dc == ECH - 1):
                              e.matmul(out=o, lhsT=l, rhs=r, start=st, stop=sp),
                              inc="P" if dc == ECH - 1 else None)
                war["wg"][wp] = pGU
                war["wu"][wp] = pGU
                vg = p.o("vector", lambda e, o=g1b[:, wp * TF:(wp + 1) * TF], i=pg,
                         r=rbc[:, par * TF:(par + 1) * TF]:
                         e.tensor_tensor(out=o, in0=i, in1=r, op=ALU.mult),
                         w=[("P", pGU), ("A", war["pgpu"][wp] and 0)], inc="V")
                aS = p.o("scalar", lambda e, o=gsb[:, wp * TF:(wp + 1) * TF],
                         i=g1b[:, wp * TF:(wp + 1) * TF], sc=sgc[:, hc:hc + 1]:
                         e.activation(out=o, in_=i, func=AF.Silu, scale=sc),
                         w=[("V", vg)], inc="A")
                p.o("vector", lambda e, o=ub[:, wp * TF:(wp + 1) * TF], i=pu,
                    r=rbc[:, par * TF:(par + 1) * TF]:
                    e.tensor_tensor(out=o, in0=i, in1=r, op=ALU.mult))
                lastV = p.o("vector", lambda e, o=hsl[:, hc * TF:(hc + 1) * TF],
                            a=gsb[:, wp * TF:(wp + 1) * TF], b=ub[:, wp * TF:(wp + 1) * TF]:
                            e.tensor_tensor(out=o, in0=a, in1=b, op=ALU.mult),
                            w=[("A", aS)], inc="V")
                war["pgpu"][wp] = lastV
            vh_last[tt] = lastV

        def plan_down(tt):
            s, c0, par = tt // 4, (tt % 4) * TF, tt % 2
            for ec in range(ECH):
                ep = ec % 2
                if build_scratch and tt == 0:
                    for gi, (h0, h1) in enumerate(((0, 16), (16, 32), (32, 44))):
                        nh = h1 - h0
                        sp_ = war["stgn"] % 2
                        war["stgn"] += 1
                        lw = p.dma("sync",
                                   stg[:, sp_ * 2048: sp_ * 2048 + nh * 128]
                                   .rearrange("p (g n) -> p g n", g=nh),
                                   d_dblk.ap()[ec, h0:h1].rearrange("g p n -> p g n"),
                                   w=[("V", war["stgV2"][sp_])], slot="stg%d" % sp_)
                        vd = p.o("vector",
                                 lambda e,
                                 o=wd[:, ep * HID + h0 * 128: ep * HID + h1 * 128]
                                 .rearrange("p (g n) -> p g n", g=nh),
                                 i=stg[:, sp_ * 2048: sp_ * 2048 + nh * 128]
                                 .rearrange("p (g n) -> p g n", g=nh),
                                 sc=suh[:, h0:h1].unsqueeze(2).broadcast_to([128, nh, 128]):
                                 e.scalar_tensor_tensor(out=o, in0=i, scalar=-128.0, in1=sc,
                                                        op0=ALU.add, op1=ALU.mult),
                                 w=[lw,
                                    ("P", war["wd"][ep]) if gi == 0 else None,
                                    war["wdst"][ep] if gi == 0 else None],
                                 inc="V")
                        war["stgV2"][sp_] = vd
                        war["stgV"] = vd
                    gst = p.dma("gpsimd", d_dsc.ap()[ec], wd[:, ep * HID:(ep + 1) * HID],
                                w=[("V", war["stgV"])], slot="dst%d" % ep)
                    war["wdst"][ep] = gst
                    dsc_stored[ec] = gst
                    lwv = [("V", war["stgV"])]
                else:
                    ld = p.dma("sync", wd[:, ep * HID:(ep + 1) * HID], d_dsc.ap()[ec],
                               w=[("P", war["wd"][ep]),
                                  dsc_stored[ec] if build_scratch else None],
                               slot="wd%d" % ep)
                    lwv = [ld]
                pd = PS[ep][:, 1536:2048]
                firstw = lwv + [("V", vh_last[tt]), ("V", war["pd"][ep])]
                pdv = None
                for hc in range(HCH):
                    pdv = p.o("tensor", lambda e, o=pd,
                              l=wd[:, ep * HID + hc * 128: ep * HID + (hc + 1) * 128],
                              r=hsl[:, hc * TF:(hc + 1) * TF],
                              st=(hc == 0), sp=(hc == HCH - 1):
                              e.matmul(out=o, lhsT=l, rhs=r, start=st, stop=sp),
                              w=firstw if hc == 0 else (),
                              inc="P" if hc == HCH - 1 else None)
                war["wd"][ep] = pdv
                xev = xep[:, ep * TF:(ep + 1) * TF]
                lx = p.dma("sync", xev, src.ap()[s, ec * 128:(ec + 1) * 128, c0:c0 + TF],
                           w=[("V", war["xep"][ep])], slot="xep%d" % ep)
                vy = p.o("vector", lambda e, o=yst[:, ep * TF:(ep + 1) * TF], i=pd,
                         sc=sdc[:, ec:ec + 1], x=xev:
                         e.scalar_tensor_tensor(out=o, in0=i, scalar=sc, in1=x,
                                                op0=ALU.mult, op1=ALU.add),
                         w=[("P", pdv), lx, war["yst_st"][ep]], inc="V")
                war["xep"][ep] = vy
                war["pd"][ep] = vy
                war["yst_st"][ep] = p.dma(
                    "gpsimd", dst.ap()[s, ec * 128:(ec + 1) * 128, c0:c0 + TF],
                    yst[:, ep * TF:(ep + 1) * TF], w=[("V", vy)], slot="yst%d" % ep)
            war["xca_pe"][par] = p.cnt["P"]

        plan_stats(0)
        for tt in range(NTT):
            plan_gateup(tt)
            plan_down(tt)
            if tt + 1 < NTT:
                plan_stats(tt + 1)
        p.barrier()
        es.close()

    def plan_attn(src, dst):
        es = ExitStack()
        s_ = lambda name, shape, dt: es.enter_context(nc.sbuf_tensor(name, shape, dt))
        xcb = s_("xcb", [128, ECH * S], BF16)
        wqb = s_("wqb", [128, 2 * E], BF16)
        wkb = s_("wkb", [128, 2 * E], BF16)
        wv4 = s_("wv4", [128, ECH * 512], BF16)
        vt4 = s_("vt4", [128, 16 * 512], BF16)
        qn0 = s_("qn0", [128, S], BF16)
        kn0 = s_("kn0", [128, S], BF16)
        xfb = s_("xfb", [128, S], F32)
        sqb2 = s_("sqb2", [128, S], BF16)
        mbcb = s_("mbcb", [128, S], BF16)
        row5 = s_("row5", [1, 512], F32)
        row5b = s_("row5b", [1, 512], F32)
        row5c = s_("row5c", [1, 512], F32)
        rowbf = s_("rowbf", [1, 512], BF16)
        rrowS = s_("rrowS", [1, S], BF16)
        r2rowS = s_("r2rowS", [1, S], BF16)
        rcols = s_("rcols", [128, 16], F32)
        sq2 = s_("sq2", [128, 4 * 512], BF16)
        cbc = s_("cbc", [128, 2 * 512], BF16)
        ptmp = s_("ptmp", [128, 512], BF16)
        m2row = s_("m2row", [1, S], BF16)
        rrow2 = s_("rrow2", [1, S], F32)
        crowr = s_("crowr", [1, S], BF16)
        ptile = s_("ptile", [128, 2 * 512], BF16)
        t_negmx = s_("t_negmx", [128, 16], F32)
        t_mx4 = s_("t_mx4", [128, 2 * 64], F32)
        t_sm = s_("t_sm", [128, 512], F32)
        t_smr = s_("t_smr", [128, 16], F32)
        t_ln = s_("t_ln", [128, 1], F32)
        t_nb = s_("t_nb", [128, 1], F32)
        bhl = s_("bhl", [128, 64], F32)
        nbhi16 = s_("nbhi16", [128, 16], BF16)
        frhs = s_("frhs", [2, S], BF16)
        aost = s_("aost", [128, 2 * 512], BF16)
        aobuf = s_("aobuf", [128, 4 * 512], BF16)
        wosb = s_("wosb", [128, 4 * 512], BF16)
        wob = s_("wob", [128, E], BF16)
        xepb = s_("xepb", [128, 2 * 512], F32)
        ystb = s_("ystb", [128, 2 * 512], F32)

        qn_bufs = (qn0, mbcb)
        kn_bufs = (kn0, sqb2)

        war = {"wq": [0, 0], "wk": [0, 0], "wv4": 0, "xfb": [0, 0], "sqb2": 0,
               "aobuf": [0, 0, 0, 0], "wosb": [0, 0, 0, 0], "wob": [None, None],
               "xepb": [0, 0], "ystb": [None, None], "bank_read": [0, 0], "bank_readA": 0,
               "rowsA": 0, "rowsP": 0, "ao_all": []}

        def plan_osc_build():
            for h in range(H):
                gc = p.dma("gpsimd",
                           wob[:].rearrange("p (g n) -> p g n", g=ECH),
                           d_oblk.ap()[h].rearrange("g p n -> p g n"),
                           w=[war["wob"][0]], slot="wob0")
                war["wob"][0] = p.dma("sync", d_osc.ap()[h], wob[:],
                                      w=[gc], slot="oscst0")

        def plan_loadx(s):
            lastP = None
            for dc in range(ECH):
                xfv = xfb[:]
                lx = p.dma("sync", xfv, src.ap()[s, dc * 128:(dc + 1) * 128, :],
                           w=[("V", war["xfb"][0])], slot="xfb0")
                vc = p.o("vector", lambda e, o=xcb[:, dc * S:(dc + 1) * S], i=xfv:
                         e.tensor_copy(out=o, in_=i), w=[lx], inc="V")
                vs = p.o("vector", lambda e, o=sqb2[:], i=xfv:
                         e.tensor_tensor(out=o, in0=i, in1=i, op=ALU.mult),
                         w=[("P", war["sqb2"])], inc="V")
                war["xfb"][0] = vs
                for tg in range(4):
                    p.o("tensor", lambda e, o=PSA[0:1, tg * 512:(tg + 1) * 512],
                        r=xcb[:, dc * S + tg * 512: dc * S + (tg + 1) * 512],
                        st=(dc == 0), sp=(dc == ECH - 1):
                        e.matmul(out=o, lhsT=ones1[:], rhs=r, start=st, stop=sp),
                        w=[("V", vc)] if tg == 0 else ())
                    lastP = p.o("tensor", lambda e, o=PSB[0:1, tg * 512:(tg + 1) * 512],
                                r=sqb2[:, tg * 512:(tg + 1) * 512],
                                st=(dc == 0), sp=(dc == ECH - 1):
                                e.matmul(out=o, lhsT=ones1[:], rhs=r, start=st, stop=sp),
                                w=[("V", vs)] if tg == 0 else (),
                                inc="P" if tg == 3 else None)
                war["sqb2"] = lastP
            pstats = lastP
            for tg in range(4):
                v1 = p.o("vector", lambda e, i=PSA[0:1, tg * 512:(tg + 1) * 512]:
                         e.tensor_scalar(out=row5[:], in0=i, scalar1=1.0 / E, scalar2=None,
                                         op0=ALU.mult),
                         w=[("P", pstats), ("P", war["rowsP"]), ("A", war["rowsA"])], inc="V")
                v1b = p.o("vector", lambda e, i=PSB[0:1, tg * 512:(tg + 1) * 512]:
                         e.tensor_scalar(out=row5b[:], in0=i, scalar1=1.0 / E, scalar2=None,
                                         op0=ALU.mult), w=[("V", v1)], inc="V")
                v1c = p.o("vector", lambda e: e.tensor_tensor(out=row5c[:], in0=row5[:], in1=row5[:],
                                                              op=ALU.mult), w=[("V", v1b)], inc="V")
                v2 = p.o("vector", lambda e: e.tensor_tensor(out=row5b[:], in0=row5b[:],
                                                             in1=row5c[:], op=ALU.subtract),
                         w=[("V", v1c)], inc="V")
                a0 = p.o("scalar", lambda e: e.activation(out=row5c[:], in_=row5b[:], func=AF.Ln,
                                                          bias=eps5[0:1, :], scale=1.0),
                         w=[("V", v2), ("P", war["rowsP"])], inc="A")
                a1 = p.o("scalar", lambda e: e.activation(out=row5c[:], in_=row5c[:],
                                                          func=AF.Exp, scale=-0.5),
                         w=[("A", a0)], inc="A")
                v3 = p.o("vector", lambda e: e.tensor_copy(out=rowbf[:], in_=row5[:]),
                         w=[("V", v2)], inc="V")
                pb1 = p.o("tensor", lambda e, o=PSA[:, tg * 512:(tg + 1) * 512]:
                          e.matmul(out=o, lhsT=onesrow[:], rhs=rowbf[:], start=True, stop=True),
                          w=[("V", v3)], inc="P")
                v5 = p.o("vector", lambda e, o=mbcb[:, tg * 512:(tg + 1) * 512],
                         i=PSA[:, tg * 512:(tg + 1) * 512]: e.tensor_copy(out=o, in_=i),
                         w=[("P", pb1)], inc="V")
                v6 = p.o("vector", lambda e, o=rrowS[0:1, tg * 512:(tg + 1) * 512]:
                         e.tensor_copy(out=o, in_=row5c[:]), w=[("A", a1)], inc="V")
                v7 = p.o("vector", lambda e, o=r2rowS[0:1, tg * 512:(tg + 1) * 512]:
                         e.tensor_tensor(out=o, in0=row5c[:], in1=row5c[:], op=ALU.mult),
                         w=[("V", v6)], inc="V")
                ptr = None
                for j in range(4):
                    tc = tg * 4 + j
                    ptr = p.o("tensor", lambda e, i=row5c[0:1, j * 128:(j + 1) * 128],
                              o=PSA[:, tg * 512: tg * 512 + 1]:
                              e.transpose(out=o, in_=i, identity=idf[0:1, 0:1]),
                              w=[("V", v5), ("A", a1)], inc="P")
                    v5 = p.o("vector", lambda e, o=rcols[:, tc:tc + 1],
                             i=PSA[:, tg * 512: tg * 512 + 1]: e.tensor_copy(out=o, in_=i),
                             w=[("P", ptr)], inc="V")
                war["rowsP"] = ptr
                war["rowsA"] = a1
            vxcb = None
            for dc in range(ECH):
                vxcb = p.o("vector", lambda e, o=xcb[:, dc * S:(dc + 1) * S]:
                           e.tensor_tensor(out=o, in0=o, in1=mbcb[:], op=ALU.subtract),
                           inc="V" if dc == ECH - 1 else None)
            return vxcb

        def new_hs(vxcb):
            return {"vxcb": vxcb,
                    "psbV": [0, 0, 0, 0], "psbA": [0, 0, 0, 0],
                    "psaV": 0, "psaA": 0,
                    "qn_done": [0, 0],
                    "qnb_rd": [0, 0],
                    "bcols_rd": [0, 0],
                    "exp1c": [0, 0, 0, 0], "vmxc": [0, 0, 0, 0], "combV": 0,
                    "nbv": 0, "lnrd": 0,
                    "sm_rd": 0, "pass1A": 0, "pass1V": 0,
                    "exp2p": [0, 0], "vao": 0, "aostore": [None, None],
                    "bbc_mm": 0, "bbc_cp": [0, 0], "vbr": 0, "vsub": [0, 0],
                    "vt_rd": 0, "smalls3": [0, 0],
                    "brow_rd": 0,
                    "pass2_end": 0}

        def issue_lv(hg):
            return p.dma("gpsimd", wv4[:].rearrange("p (m n) -> p m n", m=4 * ECH),
                         d_vblk.ap()[hg].rearrange("g j p n -> p (g j) n"),
                         w=[("P", war["wv4"])], slot="wv4")

        def plan_vt(s, hg, hs):
            lv = hs.get("lv") or issue_lv(hg)
            hs["lv"] = None
            for tc in range(16):
                tp = tc % 2
                psvt = PSB[:, tp * 512:(tp + 1) * 512]
                pv = None
                for dc in range(ECH):
                    pv = p.o("tensor", lambda e, o=psvt,
                             l=xcb[:, dc * S + tc * 128: dc * S + tc * 128 + 128],
                             r=wv4[:, dc * 512:(dc + 1) * 512],
                             st=(dc == 0), sp=(dc == ECH - 1):
                             e.matmul(out=o, lhsT=l, rhs=r, start=st, stop=sp),
                             w=[lv, ("V", hs["vxcb"]), ("V", hs["psbV"][tp]),
                                ("A", hs["psbA"][tp]), ("P", hs["pass2_end"] if tc < 2 else 0)]
                             if dc == 0 else (),
                             inc="P" if dc == ECH - 1 else None)
                ve = p.o("vector", lambda e, o=vt4[:, tc * 512:(tc + 1) * 512], i=psvt,
                         sc=rcols[:, tc:tc + 1]:
                         e.tensor_scalar(out=o, in0=i, scalar1=sc, scalar2=None,
                                         op0=ALU.mult),
                         w=[("P", pv)], inc="V")
                hs["psbV"][tp] = ve
            war["wv4"] = p.cnt["P"]

        def plan_qkv_gen(s, h, hs):
            hp = h % 2
            lq = p.dma("gpsimd",
                       wqb[:, hp * E:(hp + 1) * E].rearrange("p (g n) -> p g n", g=ECH),
                       d_qblk.ap()[h].rearrange("g p n -> p g n"),
                       w=[("P", war["wq"][hp])], slot="wq%d" % hp)
            lk = p.dma("gpsimd",
                       wkb[:, hp * E:(hp + 1) * E].rearrange("p (g n) -> p g n", g=ECH),
                       d_kblk.ap()[h].rearrange("g p n -> p g n"),
                       w=[("P", war["wk"][hp])], slot="wk%d" % hp)
            if h % 4 == 0 and h > 0:
                hs["lv"] = issue_lv(h // 4)
            qdone = 0
            for (wbuf, lw, nbuf, scl, wark) in ((wqb, lq, qn_bufs[hp], qsc[:], "wq"),
                                                (wkb, lk, kn_bufs[hp], ksc[:], "wk")):
                pq = [None] * 4
                for dc in range(ECH):
                    for tg in range(3):
                        pq[tg] = p.o("tensor", lambda e, o=PSB[:, tg * 512:(tg + 1) * 512],
                                     l=wbuf[:, hp * E + dc * 128: hp * E + (dc + 1) * 128],
                                     r=xcb[:, dc * S + tg * 512: dc * S + (tg + 1) * 512],
                                     st=(dc == 0), sp=(dc == ECH - 1):
                                     e.matmul(out=o, lhsT=l, rhs=r, start=st, stop=sp),
                                     w=[lw, ("V", hs["vxcb"]), ("P", hs["qnb_rd"][hp]),
                                        ("V", hs["psbV"][tg]), ("A", hs["psbA"][tg])]
                                     if dc == 0 else (),
                                     inc="P" if dc == ECH - 1 else None)
                    if dc % 4 == 3:
                        yield
                def tg_wave(tgs, vcr_off):
                    nonlocal qdone
                    asql = {}
                    for tg in tgs:
                        bk = tg if tg < 3 else 0
                        asql[tg] = p.o("scalar", lambda e, o=sq2[:, tg * 512:(tg + 1) * 512],
                                       i=PSB[:, bk * 512:(bk + 1) * 512]:
                                       e.activation(out=o, in_=i, func=AF.Square),
                                       w=[("P", pq[tg])], inc="A")
                        hs["psbA"][bk] = asql[tg]
                    vm2 = None
                    for tg in tgs:
                        pm2 = p.o("tensor", lambda e, r=sq2[:, tg * 512:(tg + 1) * 512]:
                                  e.matmul(out=PSB[0:1, 1536:2048], lhsT=ones1[:], rhs=r,
                                           start=True, stop=True),
                                  w=[("A", asql[tg]), ("V", vm2),
                                     ("V", hs["smalls3"][0]) if tg == tgs[0] else None,
                                     ("A", hs["smalls3"][1]) if tg == tgs[0] else None],
                                  inc="P")
                        vm2 = p.o("vector", lambda e, o=m2row[0:1, tg * 512:(tg + 1) * 512]:
                                  e.tensor_copy(out=o, in_=PSB[0:1, 1536:2048]),
                                  w=[("P", pm2)], inc="V")
                    n0, n1 = tgs[0] * 512, (tgs[-1] + 1) * 512
                    vtr = p.o("vector", lambda e:
                              e.tensor_tensor(out=rrow2[0:1, n0:n1], in0=m2row[0:1, n0:n1],
                                              in1=r2rowS[0:1, n0:n1], op=ALU.mult),
                              w=[("V", vm2)], inc="V")
                    a0 = p.o("scalar", lambda e: e.activation(out=rrow2[0:1, n0:n1],
                                                              in_=rrow2[0:1, n0:n1], func=AF.Ln,
                                                              bias=eps6[0:1, :], scale=1.0 / DH),
                             w=[("V", vtr)], inc="A")
                    a1 = p.o("scalar", lambda e: e.activation(out=rrow2[0:1, n0:n1],
                                                              in_=rrow2[0:1, n0:n1],
                                                              func=AF.Exp, scale=-0.5),
                             w=[("A", a0)], inc="A")
                    vcr = p.o("vector", lambda e:
                              e.tensor_tensor(out=crowr[0:1, n0:n1], in0=rrow2[0:1, n0:n1],
                                              in1=rrowS[0:1, n0:n1], op=ALU.mult),
                              w=[("A", a1), ("V", hs["lnrd"])], inc="V")
                    hs["lnrd"] = vcr
                    for tg in tgs:
                        bk = tg if tg < 3 else 0
                        tp = tg % 2
                        psq = PSB[:, bk * 512:(bk + 1) * 512]
                        pbc = p.o("tensor", lambda e, r=crowr[0:1, tg * 512:(tg + 1) * 512]:
                                  e.matmul(out=PSB[:, 1536:2048], lhsT=onesrow[:], rhs=r,
                                           start=True, stop=True),
                                  w=[("V", vcr), ("V", hs["smalls3"][0])],
                                  inc="P")
                        vcb = p.o("vector", lambda e, o=cbc[:, tp * 512:(tp + 1) * 512]:
                                  e.tensor_copy(out=o, in_=PSB[:, 1536:2048]),
                                  w=[("P", pbc)], inc="V")
                        vqn = p.o("vector", lambda e, o=nbuf[:, tg * 512:(tg + 1) * 512],
                                  i=psq, sc=scl, b=cbc[:, tp * 512:(tp + 1) * 512]:
                                  e.scalar_tensor_tensor(out=o, in0=i, scalar=sc, in1=b,
                                                         op0=ALU.mult, op1=ALU.mult),
                                  w=[("V", vcb)], inc="V")
                        hs["smalls3"] = [vqn, a1]
                        hs["psbV"][bk] = vqn
                        qdone = vqn

                tg_wave([0, 1, 2], 0)
                yield
                for dc in range(ECH):
                    pq[3] = p.o("tensor", lambda e, o=PSB[:, 0:512],
                                l=wbuf[:, hp * E + dc * 128: hp * E + (dc + 1) * 128],
                                r=xcb[:, dc * S + 3 * 512: dc * S + 4 * 512],
                                st=(dc == 0), sp=(dc == ECH - 1):
                                e.matmul(out=o, lhsT=l, rhs=r, start=st, stop=sp),
                                w=[("V", hs["psbV"][0]), ("A", hs["psbA"][0])]
                                if dc == 0 else (),
                                inc="P" if dc == ECH - 1 else None)
                    if dc % 8 == 7:
                        yield
                tg_wave([3], 0)
                yield
                war[wark][hp] = pq[3]
            hs["qn_done"][hp] = qdone

        def plan_pass1_qt(s, h, qt, hs):
            hp = h % 2
            qn = qn_bufs[hp]
            kn = kn_bufs[hp]
            for c in range(4):
                mxcol = t_mx4[:, hp * 64 + qt * 4 + c: hp * 64 + qt * 4 + c + 1]
                smacc = t_sm[:, (qt * 4 + c) * 8: (qt * 4 + c) * 8 + 1]
                ps1 = p.o("tensor", lambda e, o=PSA[:, c * 512:(c + 1) * 512],
                          l=qn[:, qt * 128:(qt + 1) * 128], r=kn[:, c * 512:(c + 1) * 512]:
                          e.matmul(out=o, lhsT=l, rhs=r, start=True, stop=True),
                          w=[("V", hs["qn_done"][hp]), ("A", hs["exp1c"][c]),
                             ("V", hs["vmxc"][c]),
                             ("A", hs["psaA"]) if c == 0 else None,
                             ("V", hs["psaV"]) if c == 0 else None],
                          inc="P")
                vmxc = p.o("vector", lambda e, o=mxcol, i=PSA[:, c * 512:(c + 1) * 512]:
                           e.reduce_max(out=o, in_=i, axis=AX.X, negate=True),
                           w=[("P", ps1), ("A", hs["exp1c"][c]),
                              ("V", hs["combV"]) if qt == 0 else None],
                           inc="V")
                hs["vmxc"][c] = vmxc
                ax = p.o("scalar", lambda e, o=ptmp[:],
                         i=PSA[:, c * 512:(c + 1) * 512], b=mxcol, a=smacc:
                         e.activation(out=o, in_=i, func=AF.Exp, bias=b, scale=1.0,
                                      accum_out=a),
                         w=[("V", vmxc), ("V", hs["combV"]) if qt == 0 and c == 0 else None],
                         inc="A")
                hs["exp1c"][c] = ax

        def plan_pass1_combine(s, h, hs):
            hp = h % 2
            mx64 = t_mx4[:, hp * 64:(hp + 1) * 64]
            mx3d = mx64.rearrange("p (q c) -> p q c", q=16)
            sm3d = t_sm[:].rearrange("p (q c x) -> p q c x", q=16, c=4)[:, :, :, 0]
            vmx = p.o("vector", lambda e: e.tensor_reduce(out=t_negmx[:], in_=mx3d, axis=AX.X,
                                                          op=ALU.min),
                      w=[("V", hs["nbv"]), ("A", hs["exp1c"][3])], inc="V")
            vdf = p.o("vector", lambda e: e.tensor_tensor(
                out=mx3d, in0=mx3d,
                in1=t_negmx[:].unsqueeze(2).broadcast_to([128, 16, 4]), op=ALU.subtract),
                      w=[("V", vmx)], inc="V")
            adf = p.o("scalar", lambda e: e.activation(out=mx64, in_=mx64, func=AF.Exp,
                                                       scale=-1.0),
                      w=[("V", vdf)], inc="A")
            vdt = p.o("vector", lambda e: e.tensor_tensor(out=mx3d, in0=mx3d, in1=sm3d,
                                                          op=ALU.mult),
                      w=[("A", adf)], inc="V")
            vsm = p.o("vector", lambda e: e.tensor_reduce(out=t_smr[:], in_=mx3d, axis=AX.X,
                                                          op=ALU.add),
                      w=[("V", vdt)], inc="V")
            aln = p.o("scalar", lambda e: e.activation(out=t_smr[:], in_=t_smr[:], func=AF.Ln),
                      w=[("V", vsm)], inc="A")
            vnb = p.o("vector", lambda e: e.tensor_tensor(out=t_negmx[:], in0=t_negmx[:],
                                                          in1=t_smr[:], op=ALU.subtract),
                      w=[("A", aln)], inc="V")
            bhl3 = bhl[:, hp * 32:(hp + 1) * 32].rearrange("p (q t) -> p q t", q=16)
            vnb2 = p.o("vector", lambda e: e.tensor_copy(out=nbhi16[:], in_=t_negmx[:]),
                       w=[("V", vnb)], inc="V")
            vnb3 = p.o("vector", lambda e, o=bhl3[:, :, 0]: e.tensor_copy(out=o, in_=nbhi16[:]),
                       w=[("V", vnb2), ("P", hs["bcols_rd"][hp])], inc="V")
            nbv = p.o("vector", lambda e, o=bhl3[:, :, 1]:
                      e.tensor_tensor(out=o, in0=t_negmx[:], in1=nbhi16[:], op=ALU.subtract),
                      w=[("V", vnb3)], inc="V")
            hs["nbv"] = nbv
            hs["combV"] = nbv

        def plan_pass2(s, h, hs):
            hp = h % 2
            qn = qn_bufs[hp]
            kn = kn_bufs[hp]
            j = h % 4

            def bbuild_steps(qg, first):
                steps = []
                for jj in range(4):
                    qt = qg * 4 + jj
                    def t_step(jj=jj, qt=qt, first=first):
                        ptr = p.o("tensor",
                                  lambda e,
                                  i=bhl[:, hp * 32 + qt * 2: hp * 32 + qt * 2 + 2],
                                  o=PSA[0:2, 1536:1664]:
                                  e.transpose(out=o, in_=i, identity=idf[:]),
                                  w=[("V", hs["nbv"]) if first and jj == 0 else None,
                                     ("A", hs["pass1A"]) if first and jj == 0 else None,
                                     ("V", hs["pass1V"]) if first and jj == 0 else None,
                                     ("V", hs["vbr"]) if True else None],
                                  inc="P")
                        hs["vbr"] = p.o("vector",
                                        lambda e, o=frhs[:, qt * 128:(qt + 1) * 128],
                                        i=PSA[0:2, 1536:1664]: e.tensor_copy(out=o, in_=i),
                                        w=[("P", ptr)], inc="V")
                    steps.append(t_step)
                return steps

            pending = bbuild_steps(0, True)
            for qg in range(4):
                gp = qg % 2
                for st_ in pending:
                    st_()
                pending = bbuild_steps(qg + 1, False) if qg + 1 < 4 else []
                vfr = hs["vbr"]
                prev = None
                pao = None
                for kc in range(16):
                    if kc >= 6 and pending:
                        pending.pop(0)()
                    pp = kc % 2
                    psT = PSA[:, pp * 512:(pp + 1) * 512]
                    p.o("tensor", lambda e, o=psT, l=kn[:, kc * 128:(kc + 1) * 128],
                        r=qn[:, qg * 512:(qg + 1) * 512]:
                        e.matmul(out=o, lhsT=l, rhs=r, start=True, stop=False),
                        w=[("A", hs["exp2p"][pp]),
                           ("A", hs["pass1A"]) if kc < 2 and qg == 0 else None,
                           ("V", hs["pass1V"]) if kc < 2 and qg == 0 else None])
                    p2 = p.o("tensor", lambda e, o=psT, r=frhs[:, qg * 512:(qg + 1) * 512]:
                             e.matmul(out=o, lhsT=ones2[:], rhs=r, start=False, stop=True),
                             w=[("V", vfr) if kc == 0 else None],
                             inc="P")
                    ax2 = p.o("scalar", lambda e, o=ptile[:, pp * 512:(pp + 1) * 512], i=psT:
                              e.activation(out=o, in_=i, func=AF.Exp),
                              w=[("P", p2)], inc="A")
                    hs["exp2p"][pp] = ax2
                    if prev is not None:
                        pkc = prev[1]
                        p.o("tensor", lambda e, o=PSA[:, 1024:1536],
                            l=vt4[:, pkc * 512 + j * 128: pkc * 512 + (j + 1) * 128],
                            r=ptile[:, (pkc % 2) * 512:((pkc % 2) + 1) * 512],
                            st=(pkc == 0):
                            e.matmul(out=o, lhsT=l, rhs=r, start=st, stop=False),
                            w=[("A", prev[0]),
                               ("V", hs["vao"]) if pkc == 0 else None])
                    prev = (ax2, kc)
                pao = p.o("tensor", lambda e, o=PSA[:, 1024:1536],
                          l=vt4[:, 15 * 512 + j * 128: 15 * 512 + (j + 1) * 128],
                          r=ptile[:, 512:1024]:
                          e.matmul(out=o, lhsT=l, rhs=r, start=False, stop=True),
                          w=[("A", prev[0])], inc="P")
                vao = p.o("vector", lambda e, o=aost[:, gp * 512:(gp + 1) * 512],
                          i=PSA[:, 1024:1536]:
                          e.tensor_copy(out=o, in_=i),
                          w=[("P", pao), hs["aostore"][gp]], inc="V")
                hs["vao"] = vao
                hs["aostore"][gp] = p.dma(
                    "gpsimd", d_aosc.ap()[h][:, qg * 512:(qg + 1) * 512],
                    aost[:, gp * 512:(gp + 1) * 512], w=[("V", vao)], slot="aost%d" % gp)
            hs["qnb_rd"][hp] = p.cnt["P"]
            hs["bcols_rd"][hp] = p.cnt["P"]
            hs["pass2_end"] = p.cnt["P"]
            hs["psaV"] = p.cnt["V"]
            hs["psaA"] = p.cnt["A"]

        def plan_wo(s):
            for qtb in range(4):
                for eg in range(4):
                    gp = (qtb * 4 + eg) % 2
                    banks = PS[gp]
                    lastp = None
                    for hh in range(H):
                        hp2 = hh % 4
                        la = p.dma("sync", aobuf[:, hp2 * 512:(hp2 + 1) * 512],
                                   d_aosc.ap()[hh][:, qtb * 512:(qtb + 1) * 512],
                                   w=[("P", war["aobuf"][hp2])] +
                                     (war["ao_all"] if (hh == 0 and qtb == 0 and eg == 0)
                                      else []), slot="aob%d" % hp2)
                        lo = p.dma("sync", wosb[:, hp2 * 512:(hp2 + 1) * 512],
                                   d_osc.ap()[hh][:, eg * 512:(eg + 1) * 512],
                                   w=[("P", war["wosb"][hp2])], slot="wos%d" % hp2)
                        for ee in range(4):
                            lastp = p.o("tensor", lambda e, o=banks[:, ee * 512:(ee + 1) * 512],
                                        l=wosb[:, hp2 * 512 + ee * 128: hp2 * 512 + (ee + 1) * 128],
                                        r=aobuf[:, hp2 * 512:(hp2 + 1) * 512],
                                        st=(hh == 0), sp=(hh == H - 1):
                                        e.matmul(out=o, lhsT=l, rhs=r, start=st, stop=sp),
                                        w=[lo, la,
                                           ("V", war["bank_read"][gp]) if (hh == 0 and ee == 0)
                                           else None,
                                           ("A", war["bank_readA"]) if (hh == 0 and ee == 0)
                                           else None] if (hh == 0 or ee == 0) else (),
                                        inc="P" if ee == 3 else None)
                        war["aobuf"][hp2] = lastp
                        war["wosb"][hp2] = lastp
                    pwv = lastp
                    for ee in range(4):
                        ec = eg * 4 + ee
                        ep2 = ee % 2
                        xev = xepb[:, ep2 * 512:(ep2 + 1) * 512]
                        lx = p.dma("sync", xev,
                                   src.ap()[s, ec * 128:(ec + 1) * 128, qtb * 512:(qtb + 1) * 512],
                                   w=[("V", war["xepb"][ep2])], slot="xep%d" % ep2)
                        vy = p.o("vector", lambda e, o=ystb[:, ep2 * 512:(ep2 + 1) * 512],
                                 i=banks[:, ee * 512:(ee + 1) * 512], x=xev:
                                 e.tensor_tensor(out=o, in0=i, in1=x, op=ALU.add),
                                 w=[("P", pwv), lx, war["ystb"][ep2]], inc="V")
                        war["xepb"][ep2] = vy
                        war["ystb"][ep2] = p.dma(
                            "gpsimd", dst.ap()[s, ec * 128:(ec + 1) * 128,
                                               qtb * 512:(qtb + 1) * 512],
                            ystb[:, ep2 * 512:(ep2 + 1) * 512], w=[("V", vy)],
                            slot="yst%d" % ep2)
                    war["bank_read"][gp] = p.cnt["V"]

        for s in range(2):
            if s == 0:
                plan_osc_build()
            vxcb = plan_loadx(s)
            hs = new_hs(vxcb)
            plan_vt(s, 0, hs)
            for _ in plan_qkv_gen(s, 0, hs):
                pass
            for h in range(H):
                gen = plan_qkv_gen(s, h + 1, hs) if h + 1 < H else iter(())
                for qt in range(16):
                    plan_pass1_qt(s, h, qt, hs)
                    next(gen, None)
                    next(gen, None)
                for _ in gen:
                    pass
                plan_pass1_combine(s, h, hs)
                hs["pass1A"] = hs["exp1c"][3]
                hs["pass1V"] = hs["nbv"]
                plan_pass2(s, h, hs)
                if h + 1 < H and (h + 1) % 4 == 0:
                    plan_vt(s, (h + 1) // 4, hs)
            war["ao_all"] = [(sname, p.cnt[sname]) for sname in ("D_aost0", "D_aost1")
                             if p.cnt.get(sname)]
            war["bank_read"] = [p.cnt["V"], p.cnt["V"]]
            war["bank_readA"] = p.cnt["A"]
            plan_wo(s)
            p.barrier()
        es.close()

    only = lambda ph: phases[-1] == ph
    if "A" in phases:
        plan_ffn(d_xT, d_yT if only("A") else d_x1, build_scratch=True, tag="A")
    if "B" in phases:
        plan_attn(d_x1 if "A" in phases else d_xT, d_yT if only("B") else d_x2)
    if "C" in phases:
        plan_ffn(d_x2 if "B" in phases else d_xT, d_yT,
                 build_scratch=("A" not in phases), tag="C")
    allw = [(sname, v) for sname, v in p.cnt.items() if sname.startswith("D_") and v > 0]
    fin_s, fin_v = allw[-1]
    p.o("sync", lambda e, s=fin_s, v=fin_v: e.wait_ge(p.sems[s], v), w=allw[:-1])
    p.emit()
    es_glob.close()
    p.es.close()
    return nc


def _prep_shared(wq_gate, wq_up, wq_down, wq, wk, wv, wo,
                 s_gate, s_up, s_down, q_scale, k_scale):
    c = np.ascontiguousarray
    gblk = c(wq_gate.T.reshape(ECH, 128, HCH, 128).transpose(2, 0, 1, 3))
    ublk = c(wq_up.T.reshape(ECH, 128, HCH, 128).transpose(2, 0, 1, 3))
    dblk = c(wq_down.T.reshape(HCH, 128, ECH, 128).transpose(2, 0, 1, 3))
    qblk = c(wq.reshape(E, E).reshape(ECH, 128, H, 128).transpose(2, 0, 1, 3))
    kblk = c(wk.reshape(E, E).reshape(ECH, 128, H, 128).transpose(2, 0, 1, 3))
    vblk = c(wv.reshape(E, E).reshape(ECH, 128, H, 128).transpose(2, 0, 1, 3)
             .reshape(H // 4, 4, ECH, 128, 128).transpose(0, 2, 1, 3, 4))
    oblk = c(wo.reshape(E, E).reshape(H, 128, ECH, 128).transpose(0, 2, 1, 3))
    sg = c(s_gate.reshape(HCH, 128).T)
    su = c(s_up.reshape(HCH, 128).T)
    sd = c(s_down.reshape(ECH, 128).T)
    qs = c(q_scale.reshape(128, 1))
    ks = c(k_scale.reshape(128, 1))
    return dict(gblk=gblk, ublk=ublk, dblk=dblk, qblk=qblk, kblk=kblk,
                vblk=vblk, oblk=oblk, sg=sg, su=su, sd=sd, qs=qs, ks=ks)


def kernel(L0_attn_x, L0_attn_wq, L0_attn_wk, L0_attn_wv, L0_attn_wo,
           L0_attn_q_scale, L0_attn_k_scale,
           L0_ffn_wq_gate, L0_ffn_s_gate, L0_ffn_wq_up, L0_ffn_s_up,
           L0_ffn_wq_down, L0_ffn_s_down):
    x = np.asarray(L0_attn_x, dtype=np.float32)
    shared = _prep_shared(np.asarray(L0_ffn_wq_gate), np.asarray(L0_ffn_wq_up),
                          np.asarray(L0_ffn_wq_down),
                          np.asarray(L0_attn_wq, dtype=np.float32),
                          np.asarray(L0_attn_wk, dtype=np.float32),
                          np.asarray(L0_attn_wv, dtype=np.float32),
                          np.asarray(L0_attn_wo, dtype=np.float32),
                          np.asarray(L0_ffn_s_gate, dtype=np.float32),
                          np.asarray(L0_ffn_s_up, dtype=np.float32),
                          np.asarray(L0_ffn_s_down, dtype=np.float32),
                          np.asarray(L0_attn_q_scale, dtype=np.float32),
                          np.asarray(L0_attn_k_scale, dtype=np.float32))
    nc = build_program()
    in_maps = []
    for cid in range(NCORES):
        xT = np.ascontiguousarray(x[2 * cid: 2 * cid + 2].transpose(0, 2, 1))
        m = dict(xT=xT)
        m.update(shared)
        in_maps.append(m)
    res = run_bass_kernel_spmd(nc, in_maps, core_ids=list(range(NCORES)))
    y = np.empty((16, S, E), dtype=np.float32)
    for cid in range(NCORES):
        y[2 * cid: 2 * cid + 2] = res.results[cid]["yT"].transpose(0, 2, 1)
    return y


# version 09
# speedup vs baseline: 1.0380x; 1.0380x over previous
import numpy as np
from contextlib import ExitStack

import concourse.bass as bass
import concourse.mybir as mybir
from concourse.bass_utils import run_bass_kernel_spmd

F32 = mybir.dt.float32
BF16 = mybir.dt.bfloat16
I32 = mybir.dt.int32
AF = mybir.ActivationFunctionType
ALU = mybir.AluOpType
AX = mybir.AxisListType

E = 2048
S = 2048
HID = 5632
H = 16
DH = 128
ECH = E // 128
HCH = HID // 128
TF = 512
NTT = (2 * S) // TF
NCORES = 8

ENGINES = ("sync", "gpsimd", "vector", "scalar", "tensor")
ENG_SEM = {"tensor": "P", "vector": "V", "scalar": "A", "gpsimd": "g"}


class Plan:

    def __init__(self, nc):
        self.nc = nc
        self.es = ExitStack()
        self.ops = {e: [] for e in ENGINES}
        self.cnt = {}
        self.sems = {}
        self.pending = {e: [] for e in ENGINES}
        for name in ("P", "V", "A", "g"):
            self.sems[name] = self.es.enter_context(nc.semaphore(name))
            self.cnt[name] = 0

    def _sem(self, name):
        if name not in self.sems:
            self.sems[name] = self.es.enter_context(self.nc.semaphore(name))
            self.cnt[name] = 0
        return name

    def o(self, eng, fn, w=(), inc=None, n=1):
        waits = [x for x in self.pending[eng] if x is not None] + \
                [x for x in w if x is not None and x[1]]
        self.pending[eng] = []
        rec = [fn, waits, inc, n]
        self.ops[eng].append(rec)
        if inc is not None:
            self.cnt[inc] += n
            return self.cnt[inc]
        return None

    def dma(self, eng, out, in_, w=(), slot="misc"):
        sem = self._sem("D_" + slot)
        self.o(eng, lambda e, o=out, i=in_: e.dma_start(out=o, in_=i),
               w=w, inc=sem, n=16)
        return (sem, self.cnt[sem])

    def barrier(self):
        for e in ENGINES:
            sem = ENG_SEM.get(e)
            if sem and self.ops[e]:
                rec = self.ops[e][-1]
                if rec[2] is None:
                    rec[2] = sem
                    rec[3] = 1
                    self.cnt[sem] += 1
        waits = [(s, v) for s, v in self.cnt.items() if v > 0]
        for e in ENGINES:
            self.pending[e] = list(waits)

    def emit(self):
        with self.nc.Block() as blk:
            for e in ENGINES:
                ops = self.ops[e]
                sems = self.sems

                def body(eng, ops=ops, sems=sems):
                    hw = {}
                    for fn, waits, inc, n in ops:
                        for s, v in waits:
                            if hw.get(s, 0) < v:
                                eng.wait_ge(sems[s], v)
                                hw[s] = v
                        inst = fn(eng)
                        if inc is not None:
                            inst.then_inc(sems[inc], n)
                getattr(blk, e)(body)


def build_program(phases=("A", "B", "C"), debug=False):
    nc = bass.Bass("TRN2", target_bir_lowering=False)

    d_xT = nc.dram_tensor("xT", [2, E, S], F32, kind="ExternalInput")
    d_gblk = nc.dram_tensor("gblk", [HCH, ECH, 128, 128], I32, kind="ExternalInput")
    d_ublk = nc.dram_tensor("ublk", [HCH, ECH, 128, 128], I32, kind="ExternalInput")
    d_dblk = nc.dram_tensor("dblk", [ECH, HCH, 128, 128], I32, kind="ExternalInput")
    d_qblk = nc.dram_tensor("qblk", [H, ECH, 128, 128], F32, kind="ExternalInput")
    d_kblk = nc.dram_tensor("kblk", [H, ECH, 128, 128], F32, kind="ExternalInput")
    d_vblk = nc.dram_tensor("vblk", [H // 4, ECH, 4, 128, 128], F32, kind="ExternalInput")
    d_oblk = nc.dram_tensor("oblk", [H, ECH, 128, 128], F32, kind="ExternalInput")
    d_sg = nc.dram_tensor("sg", [128, HCH], F32, kind="ExternalInput")
    d_su = nc.dram_tensor("su", [128, HCH], F32, kind="ExternalInput")
    d_sd = nc.dram_tensor("sd", [128, ECH], F32, kind="ExternalInput")
    d_qs = nc.dram_tensor("qs", [128, 1], F32, kind="ExternalInput")
    d_ks = nc.dram_tensor("ks", [128, 1], F32, kind="ExternalInput")
    d_yT = nc.dram_tensor("yT", [2, E, S], F32, kind="ExternalOutput")

    d_gsc = nc.dram_tensor("gsc", [HCH, 128, E], BF16, kind="Internal")
    d_usc = nc.dram_tensor("usc", [HCH, 128, E], BF16, kind="Internal")
    d_dsc = nc.dram_tensor("dsc", [ECH, 128, HID], BF16, kind="Internal")
    d_osc = nc.dram_tensor("osc", [H, 128, E], BF16, kind="Internal")
    d_aosc = nc.dram_tensor("aosc", [H, DH, S], BF16, kind="Internal")
    d_x1 = nc.dram_tensor("x1sc", [2, E, S], F32, kind="ExternalOutput" if debug else "Internal")
    d_x2 = nc.dram_tensor("x2sc", [2, E, S], F32, kind="ExternalOutput" if debug else "Internal")
    d_dbg = nc.dram_tensor("dbg", [128, 4096], F32,
                           kind="ExternalOutput" if debug else "Internal")

    p = Plan(nc)
    es_glob = ExitStack()
    sb = lambda name, shape, dt: es_glob.enter_context(nc.sbuf_tensor(name, shape, dt))

    ones1 = sb("ones1", [128, 1], BF16)
    onesrow = sb("onesrow", [1, 128], BF16)
    onesrowF = sb("onesrowF", [1, 128], F32)
    ones2 = sb("ones2", [2, 128], BF16)
    iot_r = sb("iot_r", [128, 128], F32)
    iot_c = sb("iot_c", [128, 1], F32)
    idf = sb("idf", [128, 128], F32)
    sgc = sb("sgc", [128, HCH], F32)
    suh = sb("suh", [128, HCH], F32)
    sdc = sb("sdc", [128, ECH], F32)
    qsc = sb("qsc", [128, 1], F32)
    ksc = sb("ksc", [128, 1], F32)
    eps5 = sb("eps5", [128, 1], F32)
    eps6 = sb("eps6", [128, 1], F32)

    PSA = es_glob.enter_context(nc.psum_tensor("PSA", [128, 2048], F32))
    PSB = es_glob.enter_context(nc.psum_tensor("PSB", [128, 2048], F32))
    PS = (PSA, PSB)

    p.dma("sync", sgc[:], d_sg.ap(), slot="setup")
    p.dma("sync", suh[:], d_su.ap(), slot="setup")
    p.dma("sync", sdc[:], d_sd.ap(), slot="setup")
    p.dma("sync", qsc[:], d_qs.ap(), slot="setup")
    lS = p.dma("sync", ksc[:], d_ks.ap(), slot="setup")
    p.o("gpsimd", lambda e: e.iota(iot_r[:], pattern=[[1, 128]], base=0,
                                   channel_multiplier=0, allow_small_or_imprecise_dtypes=True))
    p.o("gpsimd", lambda e: e.iota(iot_c[:], pattern=[[0, 1]], base=0,
                                   channel_multiplier=1, allow_small_or_imprecise_dtypes=True))
    p.o("gpsimd", lambda e: e.memset(ones1[:], 1.0))
    p.o("gpsimd", lambda e: e.memset(onesrow[:], 1.0))
    p.o("gpsimd", lambda e: e.memset(onesrowF[:], 1.0))
    p.o("gpsimd", lambda e: e.memset(eps5[:], 1e-5))
    p.o("gpsimd", lambda e: e.memset(eps6[:], 1e-6))
    gv = p.o("gpsimd", lambda e: e.memset(ones2[:], 1.0), inc="g")
    p.o("vector", lambda e: e.tensor_scalar(out=idf[:], in0=iot_r[:], scalar1=iot_c[:],
                                            scalar2=None, op0=ALU.is_equal), w=[("g", gv)])
    p.o("vector", lambda e: e.tensor_scalar(out=suh[:], in0=suh[:], scalar1=0.5,
                                            scalar2=None, op0=ALU.mult),
        w=[lS], inc="V")
    p.barrier()

    def plan_ffn(src, dst, build_scratch, tag=""):
        es = ExitStack()
        s_ = lambda name, shape, dt: es.enter_context(nc.sbuf_tensor(name + tag, shape, dt))
        xca = s_("xca", [128, 2 * ECH * TF], BF16)
        hsl = s_("hsl", [128, HCH * TF], BF16)
        wg = s_("wg", [128, 2 * E], BF16)
        wu = s_("wu", [128, 2 * E], BF16)
        wd = s_("wd", [128, 2 * HID], BF16)
        stg = s_("stg", [128, 2 * 2048], I32)
        xf = s_("xf", [128, 2 * TF], F32)
        sqb = s_("sqb", [128, 2 * TF], BF16)
        row_m = s_("row_m", [1, TF], F32)
        row_v = s_("row_v", [1, TF], F32)
        row_r = s_("row_r", [1, TF], F32)
        row_mbf = s_("row_mbf", [1, TF], BF16)
        row_rbf = s_("row_rbf", [1, TF], BF16)
        mbc = s_("mbc", [128, 2 * TF], BF16)
        rbc = s_("rbc", [128, 2 * TF], BF16)
        g1b = s_("g1b", [128, 2 * TF], F32)
        gsb = s_("gsb", [128, 2 * TF], BF16)
        ub = s_("ub", [128, 2 * TF], BF16)
        xep = s_("xep", [128, 2 * TF], F32)
        yst = s_("yst", [128, 2 * TF], F32)

        war = {"wg": [0, 0], "wu": [0, 0], "wd": [0, 0], "xf": [0, 0], "sqb": [0, 0],
               "pgpu": [0, 0], "pd": [0, 0], "yst_st": [None, None], "xep": [0, 0],
               "xca_pe": [0, 0], "stgV": 0, "stgV2": [0, 0], "stgn": 0,
               "rowsP": 0, "rowsA": 0,
               "wgst": [None, None], "wust": [None, None], "wdst": [None, None]}
        vh_last = [0] * NTT
        vxc = [0] * NTT
        dsc_stored = [None] * ECH
        usc_stored = [None] * HCH

        def plan_stats(tt):
            s, c0, par = tt // 4, (tt % 4) * TF, tt % 2
            xo = par * ECH * TF
            pcs = PSA[0:1, 1024:1536]
            psq = PSB[0:1, 1024:1536]
            lastP = None
            for dc in range(ECH):
                dp = dc % 2
                xfv = xf[:, dp * TF:(dp + 1) * TF]
                lx = p.dma("sync", xfv, src.ap()[s, dc * 128:(dc + 1) * 128, c0:c0 + TF],
                           w=[("P", war["xf"][dp])], slot="xf%d" % dp)
                xcv = xca[:, xo + dc * TF: xo + (dc + 1) * TF]
                vc = p.o("vector", lambda e, o=xcv, i=xfv: e.tensor_copy(out=o, in_=i),
                         w=[lx, ("P", war["xca_pe"][par]) if dc == 0 else None],
                         inc="V")
                sqv = sqb[:, dp * TF:(dp + 1) * TF]
                vs = p.o("vector", lambda e, o=sqv, i=xfv:
                         e.tensor_tensor(out=o, in0=i, in1=i, op=ALU.mult),
                         w=[("P", war["sqb"][dp])], inc="V")
                p.o("tensor", lambda e, o=pcs, r=xcv, st=(dc == 0), sp=(dc == ECH - 1):
                    e.matmul(out=o, lhsT=ones1[:], rhs=r, start=st, stop=sp),
                    w=[("V", vc)])
                lastP = p.o("tensor", lambda e, o=psq, r=sqv, st=(dc == 0), sp=(dc == ECH - 1):
                            e.matmul(out=o, lhsT=ones1[:], rhs=r, start=st, stop=sp),
                            w=[("V", vs)], inc="P")
                war["xf"][dp] = lastP
                war["sqb"][dp] = lastP
            v1 = p.o("vector", lambda e: e.tensor_scalar(out=row_m[:], in0=pcs, scalar1=1.0 / E,
                                                         scalar2=None, op0=ALU.mult),
                     w=[("P", lastP), ("P", war["rowsP"]), ("A", war["rowsA"])], inc="V")
            v1b = p.o("vector", lambda e: e.tensor_scalar(out=row_v[:], in0=psq, scalar1=1.0 / E,
                                                          scalar2=None, op0=ALU.mult),
                      w=[("V", v1)], inc="V")
            v1c = p.o("vector", lambda e: e.tensor_tensor(out=row_r[:], in0=row_m[:], in1=row_m[:],
                                                          op=ALU.mult), w=[("V", v1b)], inc="V")
            v2 = p.o("vector", lambda e: e.tensor_tensor(out=row_v[:], in0=row_v[:], in1=row_r[:],
                                                         op=ALU.subtract), w=[("V", v1c)], inc="V")
            a0 = p.o("scalar", lambda e: e.activation(out=row_r[:], in_=row_v[:], func=AF.Ln,
                                                      bias=eps5[0:1, :], scale=1.0),
                     w=[("V", v2)], inc="A")
            a1 = p.o("scalar", lambda e: e.activation(out=row_r[:], in_=row_r[:], func=AF.Exp,
                                                      scale=-0.5), w=[("A", a0)], inc="A")
            v3 = p.o("vector", lambda e: e.tensor_copy(out=row_mbf[:], in_=row_m[:]),
                     w=[("V", v2)], inc="V")
            pb1 = p.o("tensor", lambda e: e.matmul(out=PSA[:, 1024:1536], lhsT=onesrow[:],
                                                   rhs=row_mbf[:], start=True, stop=True),
                      w=[("V", v3)], inc="P")
            v5 = p.o("vector", lambda e, o=mbc[:, par * TF:(par + 1) * TF]:
                     e.tensor_copy(out=o, in_=PSA[:, 1024:1536]), w=[("P", pb1)], inc="V")
            v4 = p.o("vector", lambda e: e.tensor_copy(out=row_rbf[:], in_=row_r[:]),
                     w=[("A", a1)], inc="V")
            pb2 = p.o("tensor", lambda e: e.matmul(out=PSB[:, 1024:1536], lhsT=onesrow[:],
                                                   rhs=row_rbf[:], start=True, stop=True),
                      w=[("V", v4)], inc="P")
            p.o("vector", lambda e, o=rbc[:, par * TF:(par + 1) * TF]:
                e.tensor_copy(out=o, in_=PSB[:, 1024:1536]), w=[("P", pb2)], inc="V")
            war["rowsP"] = pb2
            war["rowsA"] = a1
            last = None
            for dc in range(ECH):
                last = p.o("vector", lambda e, o=xca[:, xo + dc * TF: xo + (dc + 1) * TF],
                           m=mbc[:, par * TF:(par + 1) * TF]:
                           e.tensor_tensor(out=o, in0=o, in1=m, op=ALU.subtract),
                           w=[("V", v5)] if dc == 0 else (),
                           inc="V" if dc == ECH - 1 else None)
            vxc[tt] = last

        def plan_gateup(tt):
            s, c0, par = tt // 4, (tt % 4) * TF, tt % 2
            xo = par * ECH * TF
            lastV = 0
            for hc in range(HCH):
                wp = hc % 2
                if build_scratch and tt == 0:
                    for (dram, wbuf, scn, wark, warst, stored) in (
                            (d_gblk, wg, d_gsc, "wg", "wgst", None),
                            (d_ublk, wu, d_usc, "wu", "wust", usc_stored)):
                        sp_ = war["stgn"] % 2
                        war["stgn"] += 1
                        stgh = stg[:, sp_ * 2048:(sp_ + 1) * 2048]
                        lw = p.dma("sync", stgh.rearrange("p (g n) -> p g n", g=ECH),
                                   dram.ap()[hc].rearrange("g p n -> p g n"),
                                   w=[("V", war["stgV2"][sp_])], slot="stg%d" % sp_)
                        vd = p.o("vector", lambda e, o=wbuf[:, wp * E:(wp + 1) * E], i=stgh:
                                 e.tensor_scalar(out=o, in0=i, scalar1=-128.0,
                                                 scalar2=None, op0=ALU.add),
                                 w=[lw, ("P", war[wark][wp]), war[warst][wp]],
                                 inc="V")
                        war["stgV2"][sp_] = vd
                        war["stgV"] = vd
                        gst = p.dma("gpsimd", scn.ap()[hc], wbuf[:, wp * E:(wp + 1) * E],
                                    w=[("V", vd)],
                                    slot=("gst%d" if scn is d_gsc else "ust%d") % wp)
                        war[warst][wp] = gst
                        if stored is not None:
                            stored[hc] = gst
                    lwv = [("V", war["stgV"])]
                else:
                    lg = p.dma("sync", wg[:, wp * E:(wp + 1) * E], d_gsc.ap()[hc],
                               w=[("P", war["wg"][wp]),
                                  usc_stored[hc] if build_scratch else None],
                               slot="wg%d" % wp)
                    lu = p.dma("sync", wu[:, wp * E:(wp + 1) * E], d_usc.ap()[hc],
                               w=[("P", war["wu"][wp])], slot="wu%d" % wp)
                    lwv = [lg, lu]
                pg = PS[wp][:, 0:512]
                pu = PS[wp][:, 512:1024]
                firstw = lwv + [("V", vxc[tt]), ("V", war["pgpu"][wp])]
                for dc in range(ECH):
                    p.o("tensor", lambda e, o=pg,
                        l=wg[:, wp * E + dc * 128: wp * E + (dc + 1) * 128],
                        r=xca[:, xo + dc * TF: xo + (dc + 1) * TF],
                        st=(dc == 0), sp=(dc == ECH - 1):
                        e.matmul(out=o, lhsT=l, rhs=r, start=st, stop=sp),
                        w=firstw if dc == 0 else ())
                pGU = None
                for dc in range(ECH):
                    pGU = p.o("tensor", lambda e, o=pu,
                              l=wu[:, wp * E + dc * 128: wp * E + (dc + 1) * 128],
                              r=xca[:, xo + dc * TF: xo + (dc + 1) * TF],
                              st=(dc == 0), sp=(dc == ECH - 1):
                              e.matmul(out=o, lhsT=l, rhs=r, start=st, stop=sp),
                              inc="P" if dc == ECH - 1 else None)
                war["wg"][wp] = pGU
                war["wu"][wp] = pGU
                vg = p.o("vector", lambda e, o=g1b[:, wp * TF:(wp + 1) * TF], i=pg,
                         r=rbc[:, par * TF:(par + 1) * TF]:
                         e.tensor_tensor(out=o, in0=i, in1=r, op=ALU.mult),
                         w=[("P", pGU), ("A", war["pgpu"][wp] and 0)], inc="V")
                aS = p.o("scalar", lambda e, o=gsb[:, wp * TF:(wp + 1) * TF],
                         i=g1b[:, wp * TF:(wp + 1) * TF], sc=sgc[:, hc:hc + 1]:
                         e.activation(out=o, in_=i, func=AF.Silu, scale=sc),
                         w=[("V", vg)], inc="A")
                p.o("vector", lambda e, o=ub[:, wp * TF:(wp + 1) * TF], i=pu,
                    r=rbc[:, par * TF:(par + 1) * TF]:
                    e.tensor_tensor(out=o, in0=i, in1=r, op=ALU.mult))
                lastV = p.o("vector", lambda e, o=hsl[:, hc * TF:(hc + 1) * TF],
                            a=gsb[:, wp * TF:(wp + 1) * TF], b=ub[:, wp * TF:(wp + 1) * TF]:
                            e.tensor_tensor(out=o, in0=a, in1=b, op=ALU.mult),
                            w=[("A", aS)], inc="V")
                war["pgpu"][wp] = lastV
            vh_last[tt] = lastV

        def plan_down(tt):
            s, c0, par = tt // 4, (tt % 4) * TF, tt % 2
            for ec in range(ECH):
                ep = ec % 2
                if build_scratch and tt == 0:
                    for gi, (h0, h1) in enumerate(((0, 16), (16, 32), (32, 44))):
                        nh = h1 - h0
                        sp_ = war["stgn"] % 2
                        war["stgn"] += 1
                        lw = p.dma("sync",
                                   stg[:, sp_ * 2048: sp_ * 2048 + nh * 128]
                                   .rearrange("p (g n) -> p g n", g=nh),
                                   d_dblk.ap()[ec, h0:h1].rearrange("g p n -> p g n"),
                                   w=[("V", war["stgV2"][sp_])], slot="stg%d" % sp_)
                        vd = p.o("vector",
                                 lambda e,
                                 o=wd[:, ep * HID + h0 * 128: ep * HID + h1 * 128]
                                 .rearrange("p (g n) -> p g n", g=nh),
                                 i=stg[:, sp_ * 2048: sp_ * 2048 + nh * 128]
                                 .rearrange("p (g n) -> p g n", g=nh),
                                 sc=suh[:, h0:h1].unsqueeze(2).broadcast_to([128, nh, 128]):
                                 e.scalar_tensor_tensor(out=o, in0=i, scalar=-128.0, in1=sc,
                                                        op0=ALU.add, op1=ALU.mult),
                                 w=[lw,
                                    ("P", war["wd"][ep]) if gi == 0 else None,
                                    war["wdst"][ep] if gi == 0 else None],
                                 inc="V")
                        war["stgV2"][sp_] = vd
                        war["stgV"] = vd
                    gst = p.dma("gpsimd", d_dsc.ap()[ec], wd[:, ep * HID:(ep + 1) * HID],
                                w=[("V", war["stgV"])], slot="dst%d" % ep)
                    war["wdst"][ep] = gst
                    dsc_stored[ec] = gst
                    lwv = [("V", war["stgV"])]
                else:
                    ld = p.dma("sync", wd[:, ep * HID:(ep + 1) * HID], d_dsc.ap()[ec],
                               w=[("P", war["wd"][ep]),
                                  dsc_stored[ec] if build_scratch else None],
                               slot="wd%d" % ep)
                    lwv = [ld]
                pd = PS[ep][:, 1536:2048]
                firstw = lwv + [("V", vh_last[tt]), ("V", war["pd"][ep])]
                pdv = None
                for hc in range(HCH):
                    pdv = p.o("tensor", lambda e, o=pd,
                              l=wd[:, ep * HID + hc * 128: ep * HID + (hc + 1) * 128],
                              r=hsl[:, hc * TF:(hc + 1) * TF],
                              st=(hc == 0), sp=(hc == HCH - 1):
                              e.matmul(out=o, lhsT=l, rhs=r, start=st, stop=sp),
                              w=firstw if hc == 0 else (),
                              inc="P" if hc == HCH - 1 else None)
                war["wd"][ep] = pdv
                xev = xep[:, ep * TF:(ep + 1) * TF]
                lx = p.dma("sync", xev, src.ap()[s, ec * 128:(ec + 1) * 128, c0:c0 + TF],
                           w=[("V", war["xep"][ep])], slot="xep%d" % ep)
                vy = p.o("vector", lambda e, o=yst[:, ep * TF:(ep + 1) * TF], i=pd,
                         sc=sdc[:, ec:ec + 1], x=xev:
                         e.scalar_tensor_tensor(out=o, in0=i, scalar=sc, in1=x,
                                                op0=ALU.mult, op1=ALU.add),
                         w=[("P", pdv), lx, war["yst_st"][ep]], inc="V")
                war["xep"][ep] = vy
                war["pd"][ep] = vy
                war["yst_st"][ep] = p.dma(
                    "gpsimd", dst.ap()[s, ec * 128:(ec + 1) * 128, c0:c0 + TF],
                    yst[:, ep * TF:(ep + 1) * TF], w=[("V", vy)], slot="yst%d" % ep)
            war["xca_pe"][par] = p.cnt["P"]

        plan_stats(0)
        for tt in range(NTT):
            plan_gateup(tt)
            plan_down(tt)
            if tt + 1 < NTT:
                plan_stats(tt + 1)
        p.barrier()
        es.close()

    def plan_attn(src, dst):
        es = ExitStack()
        s_ = lambda name, shape, dt: es.enter_context(nc.sbuf_tensor(name, shape, dt))
        xcb = s_("xcb", [128, ECH * S], BF16)
        wqb = s_("wqb", [128, 2 * E], BF16)
        wkb = s_("wkb", [128, 2 * E], BF16)
        wv4 = s_("wv4", [128, ECH * 512], BF16)
        vt4 = s_("vt4", [128, 16 * 512], BF16)
        qn0 = s_("qn0", [128, S], BF16)
        kn0 = s_("kn0", [128, S], BF16)
        xfb = s_("xfb", [128, 2 * S], F32)
        sqb2 = s_("sqb2", [128, S], BF16)
        mbcb = s_("mbcb", [128, S], BF16)
        row5 = s_("row5", [1, 512], F32)
        row5b = s_("row5b", [1, 512], F32)
        row5c = s_("row5c", [1, 512], F32)
        rowbf = s_("rowbf", [1, 512], BF16)
        rrowS = s_("rrowS", [1, S], BF16)
        r2rowS = s_("r2rowS", [1, S], BF16)
        rcols = s_("rcols", [128, 16], F32)
        sq2 = s_("sq2", [128, 2 * 512], BF16)
        rrow = s_("rrow", [1, 512], F32)
        cbc = s_("cbc", [128, 2 * 512], BF16)
        ptmp = s_("ptmp", [128, S], BF16)
        ptile = s_("ptile", [128, 2 * 512], BF16)
        t_negmx = s_("t_negmx", [128, 16], F32)
        t_mx4 = s_("t_mx4", [128, 2 * 64], F32)
        t_sm = s_("t_sm", [128, 512], F32)
        t_smr = s_("t_smr", [128, 16], F32)
        t_ln = s_("t_ln", [128, 1], F32)
        t_nb = s_("t_nb", [128, 1], F32)
        bcols = s_("bcols", [128, 32], F32)
        bhl = s_("bhl", [128, 64], F32)
        nbhi16 = s_("nbhi16", [128, 16], BF16)
        frhs = s_("frhs", [2, S], BF16)
        aost = s_("aost", [128, 2 * 512], BF16)
        aobuf = s_("aobuf", [128, 4 * 512], BF16)
        wosb = s_("wosb", [128, 4 * 512], BF16)
        wob = s_("wob", [128, 2 * E], BF16)
        xepb = s_("xepb", [128, 2 * 512], F32)
        ystb = s_("ystb", [128, 2 * 512], F32)

        qn_bufs = (qn0, mbcb)
        kn_bufs = (kn0, sqb2)

        war = {"wq": [0, 0], "wk": [0, 0], "wv4": 0, "xfb": [0, 0], "sqb2": 0,
               "aobuf": [0, 0, 0, 0], "wosb": [0, 0, 0, 0], "wob": [None, None],
               "xepb": [0, 0], "ystb": [None, None], "bank_read": [0, 0], "bank_readA": 0,
               "rowsA": 0, "rowsP": 0, "ao_all": []}

        def plan_osc_build():
            for h in range(H):
                wp = h % 2
                gc = p.dma("gpsimd",
                           wob[:, wp * E:(wp + 1) * E].rearrange("p (g n) -> p g n", g=ECH),
                           d_oblk.ap()[h].rearrange("g p n -> p g n"),
                           w=[war["wob"][wp]], slot="wob%d" % wp)
                war["wob"][wp] = p.dma("sync", d_osc.ap()[h], wob[:, wp * E:(wp + 1) * E],
                                       w=[gc], slot="oscst%d" % wp)

        def plan_loadx(s):
            lastP = None
            for dc in range(ECH):
                dp = dc % 2
                xfv = xfb[:, dp * S:(dp + 1) * S]
                lx = p.dma("sync", xfv, src.ap()[s, dc * 128:(dc + 1) * 128, :],
                           w=[("P", war["xfb"][dp])], slot="xfb%d" % dp)
                vc = p.o("vector", lambda e, o=xcb[:, dc * S:(dc + 1) * S], i=xfv:
                         e.tensor_copy(out=o, in_=i), w=[lx], inc="V")
                vs = p.o("vector", lambda e, o=sqb2[:], i=xfv:
                         e.tensor_tensor(out=o, in0=i, in1=i, op=ALU.mult),
                         w=[("P", war["sqb2"])], inc="V")
                for tg in range(4):
                    p.o("tensor", lambda e, o=PSA[0:1, tg * 512:(tg + 1) * 512],
                        r=xcb[:, dc * S + tg * 512: dc * S + (tg + 1) * 512],
                        st=(dc == 0), sp=(dc == ECH - 1):
                        e.matmul(out=o, lhsT=ones1[:], rhs=r, start=st, stop=sp),
                        w=[("V", vc)] if tg == 0 else ())
                    lastP = p.o("tensor", lambda e, o=PSB[0:1, tg * 512:(tg + 1) * 512],
                                r=sqb2[:, tg * 512:(tg + 1) * 512],
                                st=(dc == 0), sp=(dc == ECH - 1):
                                e.matmul(out=o, lhsT=ones1[:], rhs=r, start=st, stop=sp),
                                w=[("V", vs)] if tg == 0 else (),
                                inc="P" if tg == 3 else None)
                war["xfb"][dp] = lastP
                war["sqb2"] = lastP
            pstats = lastP
            for tg in range(4):
                v1 = p.o("vector", lambda e, i=PSA[0:1, tg * 512:(tg + 1) * 512]:
                         e.tensor_scalar(out=row5[:], in0=i, scalar1=1.0 / E, scalar2=None,
                                         op0=ALU.mult),
                         w=[("P", pstats), ("P", war["rowsP"]), ("A", war["rowsA"])], inc="V")
                v1b = p.o("vector", lambda e, i=PSB[0:1, tg * 512:(tg + 1) * 512]:
                         e.tensor_scalar(out=row5b[:], in0=i, scalar1=1.0 / E, scalar2=None,
                                         op0=ALU.mult), w=[("V", v1)], inc="V")
                v1c = p.o("vector", lambda e: e.tensor_tensor(out=row5c[:], in0=row5[:], in1=row5[:],
                                                              op=ALU.mult), w=[("V", v1b)], inc="V")
                v2 = p.o("vector", lambda e: e.tensor_tensor(out=row5b[:], in0=row5b[:],
                                                             in1=row5c[:], op=ALU.subtract),
                         w=[("V", v1c)], inc="V")
                a0 = p.o("scalar", lambda e: e.activation(out=row5c[:], in_=row5b[:], func=AF.Ln,
                                                          bias=eps5[0:1, :], scale=1.0),
                         w=[("V", v2), ("P", war["rowsP"])], inc="A")
                a1 = p.o("scalar", lambda e: e.activation(out=row5c[:], in_=row5c[:],
                                                          func=AF.Exp, scale=-0.5),
                         w=[("A", a0)], inc="A")
                v3 = p.o("vector", lambda e: e.tensor_copy(out=rowbf[:], in_=row5[:]),
                         w=[("V", v2)], inc="V")
                pb1 = p.o("tensor", lambda e, o=PSA[:, tg * 512:(tg + 1) * 512]:
                          e.matmul(out=o, lhsT=onesrow[:], rhs=rowbf[:], start=True, stop=True),
                          w=[("V", v3)], inc="P")
                v5 = p.o("vector", lambda e, o=mbcb[:, tg * 512:(tg + 1) * 512],
                         i=PSA[:, tg * 512:(tg + 1) * 512]: e.tensor_copy(out=o, in_=i),
                         w=[("P", pb1)], inc="V")
                v6 = p.o("vector", lambda e, o=rrowS[0:1, tg * 512:(tg + 1) * 512]:
                         e.tensor_copy(out=o, in_=row5c[:]), w=[("A", a1)], inc="V")
                v7 = p.o("vector", lambda e, o=r2rowS[0:1, tg * 512:(tg + 1) * 512]:
                         e.tensor_tensor(out=o, in0=row5c[:], in1=row5c[:], op=ALU.mult),
                         w=[("V", v6)], inc="V")
                ptr = None
                for j in range(4):
                    tc = tg * 4 + j
                    ptr = p.o("tensor", lambda e, i=row5c[0:1, j * 128:(j + 1) * 128],
                              o=PSA[:, tg * 512: tg * 512 + 1]:
                              e.transpose(out=o, in_=i, identity=idf[0:1, 0:1]),
                              w=[("V", v5), ("A", a1)], inc="P")
                    v5 = p.o("vector", lambda e, o=rcols[:, tc:tc + 1],
                             i=PSA[:, tg * 512: tg * 512 + 1]: e.tensor_copy(out=o, in_=i),
                             w=[("P", ptr)], inc="V")
                war["rowsP"] = ptr
                war["rowsA"] = a1
            vxcb = None
            for dc in range(ECH):
                vxcb = p.o("vector", lambda e, o=xcb[:, dc * S:(dc + 1) * S]:
                           e.tensor_tensor(out=o, in0=o, in1=mbcb[:], op=ALU.subtract),
                           inc="V" if dc == ECH - 1 else None)
            return vxcb

        def new_hs(vxcb):
            return {"vxcb": vxcb,
                    "psbV": [0, 0, 0, 0], "psbA": [0, 0, 0, 0],
                    "psaV": 0, "psaA": 0,
                    "qn_done": [0, 0],
                    "qnb_rd": [0, 0],
                    "bcols_rd": [0, 0],
                    "exp1c": [0, 0, 0, 0], "vmxc": [0, 0, 0, 0], "combV": 0,
                    "nbv": 0, "lnrd": 0,
                    "sm_rd": 0, "pass1A": 0, "pass1V": 0,
                    "exp2p": [0, 0], "vao": 0, "aostore": [None, None],
                    "bbc_mm": 0, "bbc_cp": [0, 0], "vbr": 0, "vsub": [0, 0],
                    "pbc_last": 0, "qdone_tmp": 0, "m2_last": [0, 0],
                    "vt_rd": 0, "smalls3": [0, 0],
                    "brow_rd": 0,
                    "pass2_end": 0}

        def issue_lv(hg):
            return p.dma("gpsimd", wv4[:].rearrange("p (m n) -> p m n", m=4 * ECH),
                         d_vblk.ap()[hg].rearrange("g j p n -> p (g j) n"),
                         w=[("P", war["wv4"])], slot="wv4")

        def plan_vt(s, hg, hs):
            lv = hs.get("lv") or issue_lv(hg)
            hs["lv"] = None
            for tc in range(16):
                tp = tc % 2
                psvt = PSB[:, tp * 512:(tp + 1) * 512]
                pv = None
                for dc in range(ECH):
                    pv = p.o("tensor", lambda e, o=psvt,
                             l=xcb[:, dc * S + tc * 128: dc * S + tc * 128 + 128],
                             r=wv4[:, dc * 512:(dc + 1) * 512],
                             st=(dc == 0), sp=(dc == ECH - 1):
                             e.matmul(out=o, lhsT=l, rhs=r, start=st, stop=sp),
                             w=[lv, ("V", hs["vxcb"]), ("V", hs["psbV"][tp]),
                                ("A", hs["psbA"][tp]), ("P", hs["pass2_end"] if tc < 2 else 0)]
                             if dc == 0 else (),
                             inc="P" if dc == ECH - 1 else None)
                ve = p.o("vector", lambda e, o=vt4[:, tc * 512:(tc + 1) * 512], i=psvt,
                         sc=rcols[:, tc:tc + 1]:
                         e.tensor_scalar(out=o, in0=i, scalar1=sc, scalar2=None,
                                         op0=ALU.mult),
                         w=[("P", pv)], inc="V")
                hs["psbV"][tp] = ve
            war["wv4"] = p.cnt["P"]

        def plan_qkv_gen(s, h, hs):
            hp = h % 2
            lq = p.dma("gpsimd",
                       wqb[:, hp * E:(hp + 1) * E].rearrange("p (g n) -> p g n", g=ECH),
                       d_qblk.ap()[h].rearrange("g p n -> p g n"),
                       w=[("P", war["wq"][hp])], slot="wq%d" % hp)
            lk = p.dma("gpsimd",
                       wkb[:, hp * E:(hp + 1) * E].rearrange("p (g n) -> p g n", g=ECH),
                       d_kblk.ap()[h].rearrange("g p n -> p g n"),
                       w=[("P", war["wk"][hp])], slot="wk%d" % hp)
            if h % 4 == 0 and h > 0:
                hs["lv"] = issue_lv(h // 4)
            qdone = 0
            for (wbuf, lw, nbuf, scl, wark) in ((wqb, lq, qn_bufs[hp], qsc[:], "wq"),
                                                (wkb, lk, kn_bufs[hp], ksc[:], "wk")):
                pq = [None] * 4
                chain_steps = []

                def emit_mms(tg):
                    bk = tg if tg < 3 else 0
                    for dc in range(ECH):
                        pq[tg] = p.o("tensor", lambda e, o=PSB[:, bk * 512:(bk + 1) * 512],
                                     l=wbuf[:, hp * E + dc * 128: hp * E + (dc + 1) * 128],
                                     r=xcb[:, dc * S + tg * 512: dc * S + (tg + 1) * 512],
                                     st=(dc == 0), sp=(dc == ECH - 1):
                                     e.matmul(out=o, lhsT=l, rhs=r, start=st, stop=sp),
                                     w=[lw, ("V", hs["vxcb"]), ("P", hs["qnb_rd"][hp]),
                                        ("V", hs["psbV"][bk]), ("A", hs["psbA"][bk])]
                                     if dc == 0 else (),
                                     inc="P" if dc == ECH - 1 else None)

                def chain(tg):
                    bk = tg if tg < 3 else 0
                    tp = tg % 2
                    psq = PSB[:, bk * 512:(bk + 1) * 512]
                    st = {}

                    def s_sq():
                        st["asq"] = p.o("scalar",
                                        lambda e, o=sq2[:, tp * 512:(tp + 1) * 512], i=psq:
                                        e.activation(out=o, in_=i, func=AF.Square),
                                        w=[("P", pq[tg]), ("P", hs["m2_last"][tp])], inc="A")
                        hs["psbA"][bk] = st["asq"]

                    def s_m2():
                        pm2 = p.o("tensor", lambda e, r=sq2[:, tp * 512:(tp + 1) * 512]:
                                  e.matmul(out=PSB[0:1, 1536:2048], lhsT=ones1[:], rhs=r,
                                           start=True, stop=True),
                                  w=[("A", st["asq"]), ("V", hs["smalls3"][0]),
                                     ("A", hs["smalls3"][1])],
                                  inc="P")
                        hs["m2_last"][tp] = pm2
                        vtr = p.o("vector", lambda e, r2=r2rowS[0:1, tg * 512:(tg + 1) * 512]:
                                  e.tensor_tensor(out=rrow[:], in0=PSB[0:1, 1536:2048], in1=r2,
                                                  op=ALU.mult),
                                  w=[("P", pm2), ("V", hs["lnrd"])], inc="V")
                        a0 = p.o("scalar", lambda e: e.activation(out=rrow[:], in_=rrow[:],
                                                                  func=AF.Ln,
                                                                  bias=eps6[0:1, :],
                                                                  scale=1.0 / DH),
                                 w=[("V", vtr)], inc="A")
                        a1 = p.o("scalar", lambda e: e.activation(out=rrow[:], in_=rrow[:],
                                                                  func=AF.Exp, scale=-0.5),
                                 w=[("A", a0)], inc="A")
                        st["a1"] = a1
                        st["vcr"] = p.o("vector",
                                        lambda e, rr=rrowS[0:1, tg * 512:(tg + 1) * 512]:
                                        e.tensor_tensor(out=rowbf[:], in0=rrow[:], in1=rr,
                                                        op=ALU.mult),
                                        w=[("A", a1), ("P", hs["pbc_last"])], inc="V")
                        hs["lnrd"] = st["vcr"]

                    def s_bc():
                        pbc = p.o("tensor", lambda e:
                                  e.matmul(out=PSB[:, 1536:2048], lhsT=onesrow[:],
                                           rhs=rowbf[:], start=True, stop=True),
                                  w=[("V", st["vcr"])], inc="P")
                        hs["pbc_last"] = pbc
                        vcb = p.o("vector", lambda e, o=cbc[:, tp * 512:(tp + 1) * 512]:
                                  e.tensor_copy(out=o, in_=PSB[:, 1536:2048]),
                                  w=[("P", pbc)], inc="V")
                        hs["smalls3"] = [vcb, st["a1"]]
                        vqn = p.o("vector", lambda e, o=nbuf[:, tg * 512:(tg + 1) * 512],
                                  i=psq, sc=scl, b=cbc[:, tp * 512:(tp + 1) * 512]:
                                  e.scalar_tensor_tensor(out=o, in0=i, scalar=sc, in1=b,
                                                         op0=ALU.mult, op1=ALU.mult),
                                  w=[("V", vcb)], inc="V")
                        hs["psbV"][bk] = vqn
                        hs["qdone_tmp"] = vqn

                    return [s_sq, s_m2, s_bc]

                emit_mms(0)
                yield
                c0 = chain(0)
                emit_mms(1)
                c0[0]()
                yield
                c1 = chain(1)
                emit_mms(2)
                c0[1]()
                yield
                c0[2]()
                c1[0]()
                emit_mms(3)
                yield
                c2 = chain(2)
                c1[1]()
                yield
                c1[2]()
                c2[0]()
                yield
                c2[1]()
                yield
                c2[2]()
                c3 = chain(3)
                c3[0]()
                yield
                c3[1]()
                yield
                c3[2]()
                qdone = hs["qdone_tmp"]
                yield
                war[wark][hp] = pq[3]
            hs["qn_done"][hp] = qdone

        def plan_pass1_qt(s, h, qt, hs):
            hp = h % 2
            qn = qn_bufs[hp]
            kn = kn_bufs[hp]
            for c in range(4):
                mxcol = t_mx4[:, hp * 64 + qt * 4 + c: hp * 64 + qt * 4 + c + 1]
                smacc = t_sm[:, (qt * 4 + c) * 8: (qt * 4 + c) * 8 + 1]
                ps1 = p.o("tensor", lambda e, o=PSA[:, c * 512:(c + 1) * 512],
                          l=qn[:, qt * 128:(qt + 1) * 128], r=kn[:, c * 512:(c + 1) * 512]:
                          e.matmul(out=o, lhsT=l, rhs=r, start=True, stop=True),
                          w=[("V", hs["qn_done"][hp]), ("A", hs["exp1c"][c]),
                             ("V", hs["vmxc"][c]),
                             ("A", hs["psaA"]) if c == 0 else None,
                             ("V", hs["psaV"]) if c == 0 else None],
                          inc="P")
                vmxc = p.o("vector", lambda e, o=mxcol, i=PSA[:, c * 512:(c + 1) * 512]:
                           e.reduce_max(out=o, in_=i, axis=AX.X, negate=True),
                           w=[("P", ps1), ("A", hs["exp1c"][c]),
                              ("V", hs["combV"]) if qt == 0 else None],
                           inc="V")
                hs["vmxc"][c] = vmxc
                ax = p.o("scalar", lambda e, o=ptmp[:, c * 512:(c + 1) * 512],
                         i=PSA[:, c * 512:(c + 1) * 512], b=mxcol, a=smacc:
                         e.activation(out=o, in_=i, func=AF.Exp, bias=b, scale=1.0,
                                      accum_out=a),
                         w=[("V", vmxc), ("V", hs["combV"]) if qt == 0 and c == 0 else None],
                         inc="A")
                hs["exp1c"][c] = ax

        def plan_pass1_combine(s, h, hs):
            hp = h % 2
            mx64 = t_mx4[:, hp * 64:(hp + 1) * 64]
            mx3d = mx64.rearrange("p (q c) -> p q c", q=16)
            sm3d = t_sm[:].rearrange("p (q c x) -> p q c x", q=16, c=4)[:, :, :, 0]
            vmx = p.o("vector", lambda e: e.tensor_reduce(out=t_negmx[:], in_=mx3d, axis=AX.X,
                                                          op=ALU.min),
                      w=[("V", hs["nbv"]), ("A", hs["exp1c"][3])], inc="V")
            vdf = p.o("vector", lambda e: e.tensor_tensor(
                out=mx3d, in0=mx3d,
                in1=t_negmx[:].unsqueeze(2).broadcast_to([128, 16, 4]), op=ALU.subtract),
                      w=[("V", vmx)], inc="V")
            adf = p.o("scalar", lambda e: e.activation(out=mx64, in_=mx64, func=AF.Exp,
                                                       scale=-1.0),
                      w=[("V", vdf)], inc="A")
            vdt = p.o("vector", lambda e: e.tensor_tensor(out=mx3d, in0=mx3d, in1=sm3d,
                                                          op=ALU.mult),
                      w=[("A", adf)], inc="V")
            vsm = p.o("vector", lambda e: e.tensor_reduce(out=t_smr[:], in_=mx3d, axis=AX.X,
                                                          op=ALU.add),
                      w=[("V", vdt)], inc="V")
            aln = p.o("scalar", lambda e: e.activation(out=t_smr[:], in_=t_smr[:], func=AF.Ln),
                      w=[("V", vsm)], inc="A")
            vnb = p.o("vector", lambda e: e.tensor_tensor(out=t_negmx[:], in0=t_negmx[:],
                                                          in1=t_smr[:], op=ALU.subtract),
                      w=[("A", aln)], inc="V")
            bhl3 = bhl[:, hp * 32:(hp + 1) * 32].rearrange("p (q t) -> p q t", q=16)
            vnb2 = p.o("vector", lambda e: e.tensor_copy(out=nbhi16[:], in_=t_negmx[:]),
                       w=[("V", vnb)], inc="V")
            vnb3 = p.o("vector", lambda e, o=bhl3[:, :, 0]: e.tensor_copy(out=o, in_=nbhi16[:]),
                       w=[("V", vnb2), ("P", hs["bcols_rd"][hp])], inc="V")
            nbv = p.o("vector", lambda e, o=bhl3[:, :, 1]:
                      e.tensor_tensor(out=o, in0=t_negmx[:], in1=nbhi16[:], op=ALU.subtract),
                      w=[("V", vnb3)], inc="V")
            hs["nbv"] = nbv
            hs["combV"] = nbv

        def plan_pass2(s, h, hs):
            hp = h % 2
            qn = qn_bufs[hp]
            kn = kn_bufs[hp]
            j = h % 4

            def bbuild_steps(qg, first):
                steps = []
                for jj in range(4):
                    qt = qg * 4 + jj
                    def t_step(jj=jj, qt=qt, first=first):
                        ptr = p.o("tensor",
                                  lambda e,
                                  i=bhl[:, hp * 32 + qt * 2: hp * 32 + qt * 2 + 2],
                                  o=PSA[0:2, 1536:1664]:
                                  e.transpose(out=o, in_=i, identity=idf[:]),
                                  w=[("V", hs["nbv"]) if first and jj == 0 else None,
                                     ("A", hs["pass1A"]) if first and jj == 0 else None,
                                     ("V", hs["pass1V"]) if first and jj == 0 else None,
                                     ("V", hs["vbr"]) if True else None],
                                  inc="P")
                        hs["vbr"] = p.o("vector",
                                        lambda e, o=frhs[:, qt * 128:(qt + 1) * 128],
                                        i=PSA[0:2, 1536:1664]: e.tensor_copy(out=o, in_=i),
                                        w=[("P", ptr)], inc="V")
                    steps.append(t_step)
                return steps

            pending = bbuild_steps(0, True)
            for qg in range(4):
                gp = qg % 2
                for st_ in pending:
                    st_()
                pending = bbuild_steps(qg + 1, False) if qg + 1 < 4 else []
                vfr = hs["vbr"]
                prev = None
                pao = None
                for kc in range(16):
                    if kc >= 6 and pending:
                        pending.pop(0)()
                    pp = kc % 2
                    psT = PSA[:, pp * 512:(pp + 1) * 512]
                    p.o("tensor", lambda e, o=psT, l=kn[:, kc * 128:(kc + 1) * 128],
                        r=qn[:, qg * 512:(qg + 1) * 512]:
                        e.matmul(out=o, lhsT=l, rhs=r, start=True, stop=False),
                        w=[("A", hs["exp2p"][pp]),
                           ("A", hs["pass1A"]) if kc < 2 and qg == 0 else None,
                           ("V", hs["pass1V"]) if kc < 2 and qg == 0 else None])
                    p2 = p.o("tensor", lambda e, o=psT, r=frhs[:, qg * 512:(qg + 1) * 512]:
                             e.matmul(out=o, lhsT=ones2[:], rhs=r, start=False, stop=True),
                             w=[("V", vfr) if kc == 0 else None],
                             inc="P")
                    ax2 = p.o("scalar", lambda e, o=ptile[:, pp * 512:(pp + 1) * 512], i=psT:
                              e.activation(out=o, in_=i, func=AF.Exp),
                              w=[("P", p2)], inc="A")
                    hs["exp2p"][pp] = ax2
                    if prev is not None:
                        pkc = prev[1]
                        p.o("tensor", lambda e, o=PSA[:, 1024:1536],
                            l=vt4[:, pkc * 512 + j * 128: pkc * 512 + (j + 1) * 128],
                            r=ptile[:, (pkc % 2) * 512:((pkc % 2) + 1) * 512],
                            st=(pkc == 0):
                            e.matmul(out=o, lhsT=l, rhs=r, start=st, stop=False),
                            w=[("A", prev[0]),
                               ("V", hs["vao"]) if pkc == 0 else None])
                    prev = (ax2, kc)
                pao = p.o("tensor", lambda e, o=PSA[:, 1024:1536],
                          l=vt4[:, 15 * 512 + j * 128: 15 * 512 + (j + 1) * 128],
                          r=ptile[:, 512:1024]:
                          e.matmul(out=o, lhsT=l, rhs=r, start=False, stop=True),
                          w=[("A", prev[0])], inc="P")
                vao = p.o("vector", lambda e, o=aost[:, gp * 512:(gp + 1) * 512],
                          i=PSA[:, 1024:1536]:
                          e.tensor_copy(out=o, in_=i),
                          w=[("P", pao), hs["aostore"][gp]], inc="V")
                hs["vao"] = vao
                hs["aostore"][gp] = p.dma(
                    "gpsimd", d_aosc.ap()[h][:, qg * 512:(qg + 1) * 512],
                    aost[:, gp * 512:(gp + 1) * 512], w=[("V", vao)], slot="aost%d" % gp)
            hs["qnb_rd"][hp] = p.cnt["P"]
            hs["bcols_rd"][hp] = p.cnt["P"]
            hs["pass2_end"] = p.cnt["P"]
            hs["psaV"] = p.cnt["V"]
            hs["psaA"] = p.cnt["A"]

        def plan_wo(s):
            for qtb in range(4):
                for eg in range(4):
                    gp = (qtb * 4 + eg) % 2
                    banks = PS[gp]
                    lastp = None
                    for hh in range(H):
                        hp2 = hh % 4
                        la = p.dma("sync", aobuf[:, hp2 * 512:(hp2 + 1) * 512],
                                   d_aosc.ap()[hh][:, qtb * 512:(qtb + 1) * 512],
                                   w=[("P", war["aobuf"][hp2])] +
                                     (war["ao_all"] if (hh == 0 and qtb == 0 and eg == 0)
                                      else []), slot="aob%d" % hp2)
                        lo = p.dma("sync", wosb[:, hp2 * 512:(hp2 + 1) * 512],
                                   d_osc.ap()[hh][:, eg * 512:(eg + 1) * 512],
                                   w=[("P", war["wosb"][hp2])], slot="wos%d" % hp2)
                        for ee in range(4):
                            lastp = p.o("tensor", lambda e, o=banks[:, ee * 512:(ee + 1) * 512],
                                        l=wosb[:, hp2 * 512 + ee * 128: hp2 * 512 + (ee + 1) * 128],
                                        r=aobuf[:, hp2 * 512:(hp2 + 1) * 512],
                                        st=(hh == 0), sp=(hh == H - 1):
                                        e.matmul(out=o, lhsT=l, rhs=r, start=st, stop=sp),
                                        w=[lo, la,
                                           ("V", war["bank_read"][gp]) if (hh == 0 and ee == 0)
                                           else None,
                                           ("A", war["bank_readA"]) if (hh == 0 and ee == 0)
                                           else None] if (hh == 0 or ee == 0) else (),
                                        inc="P" if ee == 3 else None)
                        war["aobuf"][hp2] = lastp
                        war["wosb"][hp2] = lastp
                    pwv = lastp
                    for ee in range(4):
                        ec = eg * 4 + ee
                        ep2 = ee % 2
                        xev = xepb[:, ep2 * 512:(ep2 + 1) * 512]
                        lx = p.dma("sync", xev,
                                   src.ap()[s, ec * 128:(ec + 1) * 128, qtb * 512:(qtb + 1) * 512],
                                   w=[("V", war["xepb"][ep2])], slot="xep%d" % ep2)
                        vy = p.o("vector", lambda e, o=ystb[:, ep2 * 512:(ep2 + 1) * 512],
                                 i=banks[:, ee * 512:(ee + 1) * 512], x=xev:
                                 e.tensor_tensor(out=o, in0=i, in1=x, op=ALU.add),
                                 w=[("P", pwv), lx, war["ystb"][ep2]], inc="V")
                        war["xepb"][ep2] = vy
                        war["ystb"][ep2] = p.dma(
                            "gpsimd", dst.ap()[s, ec * 128:(ec + 1) * 128,
                                               qtb * 512:(qtb + 1) * 512],
                            ystb[:, ep2 * 512:(ep2 + 1) * 512], w=[("V", vy)],
                            slot="yst%d" % ep2)
                    war["bank_read"][gp] = p.cnt["V"]

        for s in range(2):
            if s == 0:
                plan_osc_build()
            vxcb = plan_loadx(s)
            hs = new_hs(vxcb)
            plan_vt(s, 0, hs)
            for _ in plan_qkv_gen(s, 0, hs):
                pass
            for h in range(H):
                gen = plan_qkv_gen(s, h + 1, hs) if h + 1 < H else iter(())
                for qt in range(16):
                    plan_pass1_qt(s, h, qt, hs)
                    next(gen, None)
                    next(gen, None)
                for _ in gen:
                    pass
                plan_pass1_combine(s, h, hs)
                hs["pass1A"] = hs["exp1c"][3]
                hs["pass1V"] = hs["nbv"]
                plan_pass2(s, h, hs)
                if h + 1 < H and (h + 1) % 4 == 0:
                    plan_vt(s, (h + 1) // 4, hs)
            war["ao_all"] = [(sname, p.cnt[sname]) for sname in ("D_aost0", "D_aost1")
                             if p.cnt.get(sname)]
            war["bank_read"] = [p.cnt["V"], p.cnt["V"]]
            war["bank_readA"] = p.cnt["A"]
            plan_wo(s)
            p.barrier()
        es.close()

    only = lambda ph: phases[-1] == ph
    if "A" in phases:
        plan_ffn(d_xT, d_yT if only("A") else d_x1, build_scratch=True, tag="A")
    if "B" in phases:
        plan_attn(d_x1 if "A" in phases else d_xT, d_yT if only("B") else d_x2)
    if "C" in phases:
        plan_ffn(d_x2 if "B" in phases else d_xT, d_yT,
                 build_scratch=("A" not in phases), tag="C")
    allw = [(sname, v) for sname, v in p.cnt.items() if sname.startswith("D_") and v > 0]
    fin_s, fin_v = allw[-1]
    p.o("sync", lambda e, s=fin_s, v=fin_v: e.wait_ge(p.sems[s], v), w=allw[:-1])
    p.emit()
    es_glob.close()
    p.es.close()
    return nc


def _prep_shared(wq_gate, wq_up, wq_down, wq, wk, wv, wo,
                 s_gate, s_up, s_down, q_scale, k_scale):
    c = np.ascontiguousarray
    gblk = c(wq_gate.T.reshape(ECH, 128, HCH, 128).transpose(2, 0, 1, 3))
    ublk = c(wq_up.T.reshape(ECH, 128, HCH, 128).transpose(2, 0, 1, 3))
    dblk = c(wq_down.T.reshape(HCH, 128, ECH, 128).transpose(2, 0, 1, 3))
    qblk = c(wq.reshape(E, E).reshape(ECH, 128, H, 128).transpose(2, 0, 1, 3))
    kblk = c(wk.reshape(E, E).reshape(ECH, 128, H, 128).transpose(2, 0, 1, 3))
    vblk = c(wv.reshape(E, E).reshape(ECH, 128, H, 128).transpose(2, 0, 1, 3)
             .reshape(H // 4, 4, ECH, 128, 128).transpose(0, 2, 1, 3, 4))
    oblk = c(wo.reshape(E, E).reshape(H, 128, ECH, 128).transpose(0, 2, 1, 3))
    sg = c(s_gate.reshape(HCH, 128).T)
    su = c(s_up.reshape(HCH, 128).T)
    sd = c(s_down.reshape(ECH, 128).T)
    qs = c(q_scale.reshape(128, 1))
    ks = c(k_scale.reshape(128, 1))
    return dict(gblk=gblk, ublk=ublk, dblk=dblk, qblk=qblk, kblk=kblk,
                vblk=vblk, oblk=oblk, sg=sg, su=su, sd=sd, qs=qs, ks=ks)


def kernel(L0_attn_x, L0_attn_wq, L0_attn_wk, L0_attn_wv, L0_attn_wo,
           L0_attn_q_scale, L0_attn_k_scale,
           L0_ffn_wq_gate, L0_ffn_s_gate, L0_ffn_wq_up, L0_ffn_s_up,
           L0_ffn_wq_down, L0_ffn_s_down):
    x = np.asarray(L0_attn_x, dtype=np.float32)
    shared = _prep_shared(np.asarray(L0_ffn_wq_gate), np.asarray(L0_ffn_wq_up),
                          np.asarray(L0_ffn_wq_down),
                          np.asarray(L0_attn_wq, dtype=np.float32),
                          np.asarray(L0_attn_wk, dtype=np.float32),
                          np.asarray(L0_attn_wv, dtype=np.float32),
                          np.asarray(L0_attn_wo, dtype=np.float32),
                          np.asarray(L0_ffn_s_gate, dtype=np.float32),
                          np.asarray(L0_ffn_s_up, dtype=np.float32),
                          np.asarray(L0_ffn_s_down, dtype=np.float32),
                          np.asarray(L0_attn_q_scale, dtype=np.float32),
                          np.asarray(L0_attn_k_scale, dtype=np.float32))
    nc = build_program()
    in_maps = []
    for cid in range(NCORES):
        xT = np.ascontiguousarray(x[2 * cid: 2 * cid + 2].transpose(0, 2, 1))
        m = dict(xT=xT)
        m.update(shared)
        in_maps.append(m)
    res = run_bass_kernel_spmd(nc, in_maps, core_ids=list(range(NCORES)))
    y = np.empty((16, S, E), dtype=np.float32)
    for cid in range(NCORES):
        y[2 * cid: 2 * cid + 2] = res.results[cid]["yT"].transpose(0, 2, 1)
    return y


# version 10
# speedup vs baseline: 1.0687x; 1.0296x over previous
import numpy as np
from contextlib import ExitStack

import concourse.bass as bass
import concourse.mybir as mybir
from concourse.bass_utils import run_bass_kernel_spmd

F32 = mybir.dt.float32
BF16 = mybir.dt.bfloat16
I32 = mybir.dt.int32
AF = mybir.ActivationFunctionType
ALU = mybir.AluOpType
AX = mybir.AxisListType

E = 2048
S = 2048
HID = 5632
H = 16
DH = 128
ECH = E // 128
HCH = HID // 128
TF = 512
NTT = (2 * S) // TF
NCORES = 8

ENGINES = ("sync", "gpsimd", "vector", "scalar", "tensor")
ENG_SEM = {"tensor": "P", "vector": "V", "scalar": "A", "gpsimd": "g"}


class Plan:

    def __init__(self, nc):
        self.nc = nc
        self.es = ExitStack()
        self.ops = {e: [] for e in ENGINES}
        self.cnt = {}
        self.sems = {}
        self.pending = {e: [] for e in ENGINES}
        for name in ("P", "V", "A", "g"):
            self.sems[name] = self.es.enter_context(nc.semaphore(name))
            self.cnt[name] = 0

    def _sem(self, name):
        if name not in self.sems:
            self.sems[name] = self.es.enter_context(self.nc.semaphore(name))
            self.cnt[name] = 0
        return name

    def o(self, eng, fn, w=(), inc=None, n=1):
        waits = [x for x in self.pending[eng] if x is not None] + \
                [x for x in w if x is not None and x[1]]
        self.pending[eng] = []
        rec = [fn, waits, inc, n]
        self.ops[eng].append(rec)
        if inc is not None:
            self.cnt[inc] += n
            return self.cnt[inc]
        return None

    def dma(self, eng, out, in_, w=(), slot="misc"):
        sem = self._sem("D_" + slot)
        self.o(eng, lambda e, o=out, i=in_: e.dma_start(out=o, in_=i),
               w=w, inc=sem, n=16)
        return (sem, self.cnt[sem])

    def barrier(self):
        for e in ENGINES:
            sem = ENG_SEM.get(e)
            if sem and self.ops[e]:
                rec = self.ops[e][-1]
                if rec[2] is None:
                    rec[2] = sem
                    rec[3] = 1
                    self.cnt[sem] += 1
        waits = [(s, v) for s, v in self.cnt.items() if v > 0]
        for e in ENGINES:
            self.pending[e] = list(waits)

    def emit(self):
        with self.nc.Block() as blk:
            for e in ENGINES:
                ops = self.ops[e]
                sems = self.sems

                def body(eng, ops=ops, sems=sems):
                    hw = {}
                    for fn, waits, inc, n in ops:
                        for s, v in waits:
                            if hw.get(s, 0) < v:
                                eng.wait_ge(sems[s], v)
                                hw[s] = v
                        inst = fn(eng)
                        if inc is not None:
                            inst.then_inc(sems[inc], n)
                getattr(blk, e)(body)


def build_program(phases=("A", "B", "C"), debug=False):
    nc = bass.Bass("TRN2", target_bir_lowering=False)

    d_xT = nc.dram_tensor("xT", [2, E, S], F32, kind="ExternalInput")
    d_gblk = nc.dram_tensor("gblk", [HCH, ECH, 128, 128], I32, kind="ExternalInput")
    d_ublk = nc.dram_tensor("ublk", [HCH, ECH, 128, 128], I32, kind="ExternalInput")
    d_dblk = nc.dram_tensor("dblk", [ECH, HCH, 128, 128], I32, kind="ExternalInput")
    d_qblk = nc.dram_tensor("qblk", [H, ECH, 128, 128], F32, kind="ExternalInput")
    d_kblk = nc.dram_tensor("kblk", [H, ECH, 128, 128], F32, kind="ExternalInput")
    d_vblk = nc.dram_tensor("vblk", [H // 4, ECH, 4, 128, 128], F32, kind="ExternalInput")
    d_oblk = nc.dram_tensor("oblk", [H, ECH, 128, 128], F32, kind="ExternalInput")
    d_sg = nc.dram_tensor("sg", [128, HCH], F32, kind="ExternalInput")
    d_su = nc.dram_tensor("su", [128, HCH], F32, kind="ExternalInput")
    d_sd = nc.dram_tensor("sd", [128, ECH], F32, kind="ExternalInput")
    d_qs = nc.dram_tensor("qs", [128, 1], F32, kind="ExternalInput")
    d_ks = nc.dram_tensor("ks", [128, 1], F32, kind="ExternalInput")
    d_yT = nc.dram_tensor("yT", [2, E, S], F32, kind="ExternalOutput")

    d_gsc = nc.dram_tensor("gsc", [HCH, 128, E], BF16, kind="Internal")
    d_usc = nc.dram_tensor("usc", [HCH, 128, E], BF16, kind="Internal")
    d_dsc = nc.dram_tensor("dsc", [ECH, 128, HID], BF16, kind="Internal")
    d_osc = nc.dram_tensor("osc", [H, 128, E], BF16, kind="Internal")
    d_aosc = nc.dram_tensor("aosc", [H, DH, S], BF16, kind="Internal")
    d_x1 = nc.dram_tensor("x1sc", [2, E, S], F32, kind="ExternalOutput" if debug else "Internal")
    d_x2 = nc.dram_tensor("x2sc", [2, E, S], F32, kind="ExternalOutput" if debug else "Internal")
    d_dbg = nc.dram_tensor("dbg", [128, 4096], F32,
                           kind="ExternalOutput" if debug else "Internal")

    p = Plan(nc)
    es_glob = ExitStack()
    sb = lambda name, shape, dt: es_glob.enter_context(nc.sbuf_tensor(name, shape, dt))

    ones1 = sb("ones1", [128, 1], BF16)
    onesrow = sb("onesrow", [1, 128], BF16)
    onesrowF = sb("onesrowF", [1, 128], F32)
    ones2 = sb("ones2", [2, 128], BF16)
    iot_r = sb("iot_r", [128, 128], F32)
    iot_c = sb("iot_c", [128, 1], F32)
    idf = sb("idf", [128, 128], F32)
    sgc = sb("sgc", [128, HCH], F32)
    suh = sb("suh", [128, HCH], F32)
    sdc = sb("sdc", [128, ECH], F32)
    qsc = sb("qsc", [128, 1], F32)
    ksc = sb("ksc", [128, 1], F32)
    eps5 = sb("eps5", [128, 1], F32)
    eps6 = sb("eps6", [128, 1], F32)

    PSA = es_glob.enter_context(nc.psum_tensor("PSA", [128, 2048], F32))
    PSB = es_glob.enter_context(nc.psum_tensor("PSB", [128, 2048], F32))
    PS = (PSA, PSB)

    p.dma("sync", sgc[:], d_sg.ap(), slot="setup")
    p.dma("sync", suh[:], d_su.ap(), slot="setup")
    p.dma("sync", sdc[:], d_sd.ap(), slot="setup")
    p.dma("sync", qsc[:], d_qs.ap(), slot="setup")
    lS = p.dma("sync", ksc[:], d_ks.ap(), slot="setup")
    p.o("gpsimd", lambda e: e.iota(iot_r[:], pattern=[[1, 128]], base=0,
                                   channel_multiplier=0, allow_small_or_imprecise_dtypes=True))
    p.o("gpsimd", lambda e: e.iota(iot_c[:], pattern=[[0, 1]], base=0,
                                   channel_multiplier=1, allow_small_or_imprecise_dtypes=True))
    p.o("gpsimd", lambda e: e.memset(ones1[:], 1.0))
    p.o("gpsimd", lambda e: e.memset(onesrow[:], 1.0))
    p.o("gpsimd", lambda e: e.memset(onesrowF[:], 1.0))
    p.o("gpsimd", lambda e: e.memset(eps5[:], 1e-5))
    p.o("gpsimd", lambda e: e.memset(eps6[:], 1e-6))
    gv = p.o("gpsimd", lambda e: e.memset(ones2[:], 1.0), inc="g")
    p.o("vector", lambda e: e.tensor_scalar(out=idf[:], in0=iot_r[:], scalar1=iot_c[:],
                                            scalar2=None, op0=ALU.is_equal), w=[("g", gv)])
    p.o("vector", lambda e: e.tensor_scalar(out=suh[:], in0=suh[:], scalar1=0.5,
                                            scalar2=None, op0=ALU.mult),
        w=[lS], inc="V")
    p.barrier()

    def plan_ffn(src, dst, build_scratch, tag=""):
        es = ExitStack()
        s_ = lambda name, shape, dt: es.enter_context(nc.sbuf_tensor(name + tag, shape, dt))
        xca = s_("xca", [128, 2 * ECH * TF], BF16)
        hsl = s_("hsl", [128, HCH * TF], BF16)
        wg = s_("wg", [128, 2 * E], BF16)
        wu = s_("wu", [128, 2 * E], BF16)
        wd = s_("wd", [128, 2 * HID], BF16)
        stg = s_("stg", [128, 2 * 2048], I32)
        xf = s_("xf", [128, 4 * TF], F32)
        sqb = s_("sqb", [128, 4 * TF], BF16)
        row_m = s_("row_m", [1, TF], F32)
        row_v = s_("row_v", [1, TF], F32)
        row_r = s_("row_r", [1, TF], F32)
        row_mbf = s_("row_mbf", [1, TF], BF16)
        row_rbf = s_("row_rbf", [1, TF], BF16)
        mbc = s_("mbc", [128, 2 * TF], BF16)
        rbc = s_("rbc", [128, 2 * TF], BF16)
        g1b = s_("g1b", [128, 2 * TF], F32)
        gsb = s_("gsb", [128, 2 * TF], BF16)
        ub = s_("ub", [128, 2 * TF], BF16)
        xep = s_("xep", [128, 2 * TF], F32)
        yst = s_("yst", [128, 2 * TF], F32)

        war = {"wg": [0, 0], "wu": [0, 0], "wd": [0, 0], "xf": [0, 0, 0, 0],
               "sqb": [0, 0, 0, 0],
               "pgpu": [0, 0], "pd": [0, 0], "yst_st": [None, None], "xep": [0, 0],
               "xca_pe": [0, 0], "stgV": 0, "stgV2": [0, 0], "stgn": 0,
               "rowsP": 0, "rowsA": 0,
               "wgst": [None, None], "wust": [None, None], "wdst": [None, None]}
        vh_last = [0] * NTT
        vxc = [0] * NTT
        dsc_stored = [None] * ECH
        usc_stored = [None] * HCH

        def plan_stats(tt):
            s, c0, par = tt // 4, (tt % 4) * TF, tt % 2
            xo = par * ECH * TF
            pcs = PSA[0:1, 1024:1536]
            psq = PSB[0:1, 1024:1536]
            lastP = None
            for dc in range(ECH):
                dp = dc % 4
                xfv = xf[:, dp * TF:(dp + 1) * TF]
                lx = p.dma("sync", xfv, src.ap()[s, dc * 128:(dc + 1) * 128, c0:c0 + TF],
                           w=[("P", war["xf"][dp])], slot="xf%d" % dp)
                xcv = xca[:, xo + dc * TF: xo + (dc + 1) * TF]
                vc = p.o("vector", lambda e, o=xcv, i=xfv: e.tensor_copy(out=o, in_=i),
                         w=[lx, ("P", war["xca_pe"][par]) if dc == 0 else None],
                         inc="V")
                sqv = sqb[:, dp * TF:(dp + 1) * TF]
                vs = p.o("vector", lambda e, o=sqv, i=xfv:
                         e.tensor_tensor(out=o, in0=i, in1=i, op=ALU.mult),
                         w=[("P", war["sqb"][dp])], inc="V")
                p.o("tensor", lambda e, o=pcs, r=xcv, st=(dc == 0), sp=(dc == ECH - 1):
                    e.matmul(out=o, lhsT=ones1[:], rhs=r, start=st, stop=sp),
                    w=[("V", vc)])
                lastP = p.o("tensor", lambda e, o=psq, r=sqv, st=(dc == 0), sp=(dc == ECH - 1):
                            e.matmul(out=o, lhsT=ones1[:], rhs=r, start=st, stop=sp),
                            w=[("V", vs)], inc="P")
                war["xf"][dp] = lastP
                war["sqb"][dp] = lastP
            v1 = p.o("vector", lambda e: e.tensor_scalar(out=row_m[:], in0=pcs, scalar1=1.0 / E,
                                                         scalar2=None, op0=ALU.mult),
                     w=[("P", lastP), ("P", war["rowsP"]), ("A", war["rowsA"])], inc="V")
            v1b = p.o("vector", lambda e: e.tensor_scalar(out=row_v[:], in0=psq, scalar1=1.0 / E,
                                                          scalar2=None, op0=ALU.mult),
                      w=[("V", v1)], inc="V")
            v1c = p.o("vector", lambda e: e.tensor_tensor(out=row_r[:], in0=row_m[:], in1=row_m[:],
                                                          op=ALU.mult), w=[("V", v1b)], inc="V")
            v2 = p.o("vector", lambda e: e.tensor_tensor(out=row_v[:], in0=row_v[:], in1=row_r[:],
                                                         op=ALU.subtract), w=[("V", v1c)], inc="V")
            a0 = p.o("scalar", lambda e: e.activation(out=row_r[:], in_=row_v[:], func=AF.Ln,
                                                      bias=eps5[0:1, :], scale=1.0),
                     w=[("V", v2)], inc="A")
            a1 = p.o("scalar", lambda e: e.activation(out=row_r[:], in_=row_r[:], func=AF.Exp,
                                                      scale=-0.5), w=[("A", a0)], inc="A")
            v3 = p.o("vector", lambda e: e.tensor_copy(out=row_mbf[:], in_=row_m[:]),
                     w=[("V", v2)], inc="V")
            pb1 = p.o("tensor", lambda e: e.matmul(out=PSA[:, 1024:1536], lhsT=onesrow[:],
                                                   rhs=row_mbf[:], start=True, stop=True),
                      w=[("V", v3)], inc="P")
            v5 = p.o("vector", lambda e, o=mbc[:, par * TF:(par + 1) * TF]:
                     e.tensor_copy(out=o, in_=PSA[:, 1024:1536]), w=[("P", pb1)], inc="V")
            v4 = p.o("vector", lambda e: e.tensor_copy(out=row_rbf[:], in_=row_r[:]),
                     w=[("A", a1)], inc="V")
            pb2 = p.o("tensor", lambda e: e.matmul(out=PSB[:, 1024:1536], lhsT=onesrow[:],
                                                   rhs=row_rbf[:], start=True, stop=True),
                      w=[("V", v4)], inc="P")
            p.o("vector", lambda e, o=rbc[:, par * TF:(par + 1) * TF]:
                e.tensor_copy(out=o, in_=PSB[:, 1024:1536]), w=[("P", pb2)], inc="V")
            war["rowsP"] = pb2
            war["rowsA"] = a1
            last = None
            for dc in range(ECH):
                last = p.o("vector", lambda e, o=xca[:, xo + dc * TF: xo + (dc + 1) * TF],
                           m=mbc[:, par * TF:(par + 1) * TF]:
                           e.tensor_tensor(out=o, in0=o, in1=m, op=ALU.subtract),
                           w=[("V", v5)] if dc == 0 else (),
                           inc="V" if dc == ECH - 1 else None)
            vxc[tt] = last

        def plan_gateup(tt):
            s, c0, par = tt // 4, (tt % 4) * TF, tt % 2
            xo = par * ECH * TF
            lastV = 0
            for hc in range(HCH):
                wp = hc % 2
                if build_scratch and tt == 0:
                    for (dram, wbuf, scn, wark, warst, stored) in (
                            (d_gblk, wg, d_gsc, "wg", "wgst", None),
                            (d_ublk, wu, d_usc, "wu", "wust", usc_stored)):
                        sp_ = war["stgn"] % 2
                        war["stgn"] += 1
                        stgh = stg[:, sp_ * 2048:(sp_ + 1) * 2048]
                        lw = p.dma("sync", stgh.rearrange("p (g n) -> p g n", g=ECH),
                                   dram.ap()[hc].rearrange("g p n -> p g n"),
                                   w=[("V", war["stgV2"][sp_])], slot="stg%d" % sp_)
                        vd = p.o("vector", lambda e, o=wbuf[:, wp * E:(wp + 1) * E], i=stgh:
                                 e.tensor_scalar(out=o, in0=i, scalar1=-128.0,
                                                 scalar2=None, op0=ALU.add),
                                 w=[lw, ("P", war[wark][wp]), war[warst][wp]],
                                 inc="V")
                        war["stgV2"][sp_] = vd
                        war["stgV"] = vd
                        gst = p.dma("gpsimd", scn.ap()[hc], wbuf[:, wp * E:(wp + 1) * E],
                                    w=[("V", vd)],
                                    slot=("gst%d" if scn is d_gsc else "ust%d") % wp)
                        war[warst][wp] = gst
                        if stored is not None:
                            stored[hc] = gst
                    lwv = [("V", war["stgV"])]
                else:
                    lg = p.dma("sync", wg[:, wp * E:(wp + 1) * E], d_gsc.ap()[hc],
                               w=[("P", war["wg"][wp]),
                                  usc_stored[hc] if build_scratch else None],
                               slot="wg%d" % wp)
                    lu = p.dma("sync", wu[:, wp * E:(wp + 1) * E], d_usc.ap()[hc],
                               w=[("P", war["wu"][wp])], slot="wu%d" % wp)
                    lwv = [lg, lu]
                pg = PS[wp][:, 0:512]
                pu = PS[wp][:, 512:1024]
                firstw = lwv + [("V", vxc[tt]), ("V", war["pgpu"][wp])]
                for dc in range(ECH):
                    p.o("tensor", lambda e, o=pg,
                        l=wg[:, wp * E + dc * 128: wp * E + (dc + 1) * 128],
                        r=xca[:, xo + dc * TF: xo + (dc + 1) * TF],
                        st=(dc == 0), sp=(dc == ECH - 1):
                        e.matmul(out=o, lhsT=l, rhs=r, start=st, stop=sp),
                        w=firstw if dc == 0 else ())
                pGU = None
                for dc in range(ECH):
                    pGU = p.o("tensor", lambda e, o=pu,
                              l=wu[:, wp * E + dc * 128: wp * E + (dc + 1) * 128],
                              r=xca[:, xo + dc * TF: xo + (dc + 1) * TF],
                              st=(dc == 0), sp=(dc == ECH - 1):
                              e.matmul(out=o, lhsT=l, rhs=r, start=st, stop=sp),
                              inc="P" if dc == ECH - 1 else None)
                war["wg"][wp] = pGU
                war["wu"][wp] = pGU
                vg = p.o("vector", lambda e, o=g1b[:, wp * TF:(wp + 1) * TF], i=pg,
                         r=rbc[:, par * TF:(par + 1) * TF]:
                         e.tensor_tensor(out=o, in0=i, in1=r, op=ALU.mult),
                         w=[("P", pGU), ("A", war["pgpu"][wp] and 0)], inc="V")
                aS = p.o("scalar", lambda e, o=gsb[:, wp * TF:(wp + 1) * TF],
                         i=g1b[:, wp * TF:(wp + 1) * TF], sc=sgc[:, hc:hc + 1]:
                         e.activation(out=o, in_=i, func=AF.Silu, scale=sc),
                         w=[("V", vg)], inc="A")
                p.o("vector", lambda e, o=ub[:, wp * TF:(wp + 1) * TF], i=pu,
                    r=rbc[:, par * TF:(par + 1) * TF]:
                    e.tensor_tensor(out=o, in0=i, in1=r, op=ALU.mult))
                lastV = p.o("vector", lambda e, o=hsl[:, hc * TF:(hc + 1) * TF],
                            a=gsb[:, wp * TF:(wp + 1) * TF], b=ub[:, wp * TF:(wp + 1) * TF]:
                            e.tensor_tensor(out=o, in0=a, in1=b, op=ALU.mult),
                            w=[("A", aS)], inc="V")
                war["pgpu"][wp] = lastV
            vh_last[tt] = lastV

        def plan_down(tt):
            s, c0, par = tt // 4, (tt % 4) * TF, tt % 2
            for ec in range(ECH):
                ep = ec % 2
                if build_scratch and tt == 0:
                    for gi, (h0, h1) in enumerate(((0, 16), (16, 32), (32, 44))):
                        nh = h1 - h0
                        sp_ = war["stgn"] % 2
                        war["stgn"] += 1
                        lw = p.dma("sync",
                                   stg[:, sp_ * 2048: sp_ * 2048 + nh * 128]
                                   .rearrange("p (g n) -> p g n", g=nh),
                                   d_dblk.ap()[ec, h0:h1].rearrange("g p n -> p g n"),
                                   w=[("V", war["stgV2"][sp_])], slot="stg%d" % sp_)
                        vd = p.o("vector",
                                 lambda e,
                                 o=wd[:, ep * HID + h0 * 128: ep * HID + h1 * 128]
                                 .rearrange("p (g n) -> p g n", g=nh),
                                 i=stg[:, sp_ * 2048: sp_ * 2048 + nh * 128]
                                 .rearrange("p (g n) -> p g n", g=nh),
                                 sc=suh[:, h0:h1].unsqueeze(2).broadcast_to([128, nh, 128]):
                                 e.scalar_tensor_tensor(out=o, in0=i, scalar=-128.0, in1=sc,
                                                        op0=ALU.add, op1=ALU.mult),
                                 w=[lw,
                                    ("P", war["wd"][ep]) if gi == 0 else None,
                                    war["wdst"][ep] if gi == 0 else None],
                                 inc="V")
                        war["stgV2"][sp_] = vd
                        war["stgV"] = vd
                    gst = p.dma("gpsimd", d_dsc.ap()[ec], wd[:, ep * HID:(ep + 1) * HID],
                                w=[("V", war["stgV"])], slot="dst%d" % ep)
                    war["wdst"][ep] = gst
                    dsc_stored[ec] = gst
                    lwv = [("V", war["stgV"])]
                else:
                    ld = p.dma("sync", wd[:, ep * HID:(ep + 1) * HID], d_dsc.ap()[ec],
                               w=[("P", war["wd"][ep]),
                                  dsc_stored[ec] if build_scratch else None],
                               slot="wd%d" % ep)
                    lwv = [ld]
                pd = PS[ep][:, 1536:2048]
                firstw = lwv + [("V", vh_last[tt]), ("V", war["pd"][ep])]
                pdv = None
                for hc in range(HCH):
                    pdv = p.o("tensor", lambda e, o=pd,
                              l=wd[:, ep * HID + hc * 128: ep * HID + (hc + 1) * 128],
                              r=hsl[:, hc * TF:(hc + 1) * TF],
                              st=(hc == 0), sp=(hc == HCH - 1):
                              e.matmul(out=o, lhsT=l, rhs=r, start=st, stop=sp),
                              w=firstw if hc == 0 else (),
                              inc="P" if hc == HCH - 1 else None)
                war["wd"][ep] = pdv
                xev = xep[:, ep * TF:(ep + 1) * TF]
                lx = p.dma("sync", xev, src.ap()[s, ec * 128:(ec + 1) * 128, c0:c0 + TF],
                           w=[("V", war["xep"][ep])], slot="xep%d" % ep)
                vy = p.o("vector", lambda e, o=yst[:, ep * TF:(ep + 1) * TF], i=pd,
                         sc=sdc[:, ec:ec + 1], x=xev:
                         e.scalar_tensor_tensor(out=o, in0=i, scalar=sc, in1=x,
                                                op0=ALU.mult, op1=ALU.add),
                         w=[("P", pdv), lx, war["yst_st"][ep]], inc="V")
                war["xep"][ep] = vy
                war["pd"][ep] = vy
                war["yst_st"][ep] = p.dma(
                    "gpsimd", dst.ap()[s, ec * 128:(ec + 1) * 128, c0:c0 + TF],
                    yst[:, ep * TF:(ep + 1) * TF], w=[("V", vy)], slot="yst%d" % ep)
            war["xca_pe"][par] = p.cnt["P"]

        plan_stats(0)
        for tt in range(NTT):
            plan_gateup(tt)
            plan_down(tt)
            if tt + 1 < NTT:
                plan_stats(tt + 1)
        p.barrier()
        es.close()

    def plan_attn(src, dst):
        es = ExitStack()
        s_ = lambda name, shape, dt: es.enter_context(nc.sbuf_tensor(name, shape, dt))
        xcb = s_("xcb", [128, ECH * S], BF16)
        wqb = s_("wqb", [128, 2 * E], BF16)
        wkb = s_("wkb", [128, 2 * E], BF16)
        wv4 = s_("wv4", [128, ECH * 512], BF16)
        vt4 = s_("vt4", [128, 16 * 512], BF16)
        qn0 = s_("qn0", [128, S], BF16)
        kn0 = s_("kn0", [128, S], BF16)
        xfb = s_("xfb", [128, 2 * S], F32)
        sqb2 = s_("sqb2", [128, S], BF16)
        mbcb = s_("mbcb", [128, S], BF16)
        row5 = s_("row5", [1, 512], F32)
        row5b = s_("row5b", [1, 512], F32)
        row5c = s_("row5c", [1, 512], F32)
        rowbf = s_("rowbf", [1, 512], BF16)
        rrowS = s_("rrowS", [1, S], BF16)
        r2rowS = s_("r2rowS", [1, S], BF16)
        rcols = s_("rcols", [128, 16], F32)
        sq2 = s_("sq2", [128, 2 * 512], BF16)
        rrow = s_("rrow", [1, 512], F32)
        cbc = s_("cbc", [128, 2 * 512], BF16)
        ptmp = s_("ptmp", [128, S], BF16)
        ptile = s_("ptile", [128, 2 * 512], BF16)
        t_negmx = s_("t_negmx", [128, 16], F32)
        t_mx4 = s_("t_mx4", [128, 2 * 64], F32)
        t_sm = s_("t_sm", [128, 512], F32)
        t_smr = s_("t_smr", [128, 16], F32)
        t_ln = s_("t_ln", [128, 1], F32)
        t_nb = s_("t_nb", [128, 1], F32)
        bcols = s_("bcols", [128, 32], F32)
        bhl = s_("bhl", [128, 64], F32)
        nbhi16 = s_("nbhi16", [128, 16], BF16)
        frhs = s_("frhs", [2, S], BF16)
        aost = s_("aost", [128, 2 * 512], BF16)
        aobuf = s_("aobuf", [128, 4 * 512], BF16)
        wosb = s_("wosb", [128, 4 * 512], BF16)
        wob = s_("wob", [128, 2 * E], BF16)
        xepb = s_("xepb", [128, 2 * 512], F32)
        ystb = s_("ystb", [128, 2 * 512], F32)

        qn_bufs = (qn0, mbcb)
        kn_bufs = (kn0, sqb2)

        war = {"wq": [0, 0], "wk": [0, 0], "wv4": 0, "xfb": [0, 0], "sqb2": 0,
               "aobuf": [0, 0, 0, 0], "wosb": [0, 0, 0, 0], "wob": [None, None],
               "xepb": [0, 0], "ystb": [None, None], "bank_read": [0, 0], "bank_readA": 0,
               "rowsA": 0, "rowsP": 0, "ao_all": []}

        def plan_osc_build():
            for h in range(H):
                wp = h % 2
                gc = p.dma("gpsimd",
                           wob[:, wp * E:(wp + 1) * E].rearrange("p (g n) -> p g n", g=ECH),
                           d_oblk.ap()[h].rearrange("g p n -> p g n"),
                           w=[war["wob"][wp]], slot="wob%d" % wp)
                war["wob"][wp] = p.dma("sync", d_osc.ap()[h], wob[:, wp * E:(wp + 1) * E],
                                       w=[gc], slot="oscst%d" % wp)

        def plan_loadx(s):
            lastP = None
            for dc in range(ECH):
                dp = dc % 2
                xfv = xfb[:, dp * S:(dp + 1) * S]
                lx = p.dma("sync", xfv, src.ap()[s, dc * 128:(dc + 1) * 128, :],
                           w=[("P", war["xfb"][dp])], slot="xfb%d" % dp)
                vc = p.o("vector", lambda e, o=xcb[:, dc * S:(dc + 1) * S], i=xfv:
                         e.tensor_copy(out=o, in_=i), w=[lx], inc="V")
                vs = p.o("vector", lambda e, o=sqb2[:], i=xfv:
                         e.tensor_tensor(out=o, in0=i, in1=i, op=ALU.mult),
                         w=[("P", war["sqb2"])], inc="V")
                for tg in range(4):
                    p.o("tensor", lambda e, o=PSA[0:1, tg * 512:(tg + 1) * 512],
                        r=xcb[:, dc * S + tg * 512: dc * S + (tg + 1) * 512],
                        st=(dc == 0), sp=(dc == ECH - 1):
                        e.matmul(out=o, lhsT=ones1[:], rhs=r, start=st, stop=sp),
                        w=[("V", vc)] if tg == 0 else ())
                    lastP = p.o("tensor", lambda e, o=PSB[0:1, tg * 512:(tg + 1) * 512],
                                r=sqb2[:, tg * 512:(tg + 1) * 512],
                                st=(dc == 0), sp=(dc == ECH - 1):
                                e.matmul(out=o, lhsT=ones1[:], rhs=r, start=st, stop=sp),
                                w=[("V", vs)] if tg == 0 else (),
                                inc="P" if tg == 3 else None)
                war["xfb"][dp] = lastP
                war["sqb2"] = lastP
            pstats = lastP
            for tg in range(4):
                v1 = p.o("vector", lambda e, i=PSA[0:1, tg * 512:(tg + 1) * 512]:
                         e.tensor_scalar(out=row5[:], in0=i, scalar1=1.0 / E, scalar2=None,
                                         op0=ALU.mult),
                         w=[("P", pstats), ("P", war["rowsP"]), ("A", war["rowsA"])], inc="V")
                v1b = p.o("vector", lambda e, i=PSB[0:1, tg * 512:(tg + 1) * 512]:
                         e.tensor_scalar(out=row5b[:], in0=i, scalar1=1.0 / E, scalar2=None,
                                         op0=ALU.mult), w=[("V", v1)], inc="V")
                v1c = p.o("vector", lambda e: e.tensor_tensor(out=row5c[:], in0=row5[:], in1=row5[:],
                                                              op=ALU.mult), w=[("V", v1b)], inc="V")
                v2 = p.o("vector", lambda e: e.tensor_tensor(out=row5b[:], in0=row5b[:],
                                                             in1=row5c[:], op=ALU.subtract),
                         w=[("V", v1c)], inc="V")
                a0 = p.o("scalar", lambda e: e.activation(out=row5c[:], in_=row5b[:], func=AF.Ln,
                                                          bias=eps5[0:1, :], scale=1.0),
                         w=[("V", v2), ("P", war["rowsP"])], inc="A")
                a1 = p.o("scalar", lambda e: e.activation(out=row5c[:], in_=row5c[:],
                                                          func=AF.Exp, scale=-0.5),
                         w=[("A", a0)], inc="A")
                v3 = p.o("vector", lambda e: e.tensor_copy(out=rowbf[:], in_=row5[:]),
                         w=[("V", v2)], inc="V")
                pb1 = p.o("tensor", lambda e, o=PSA[:, tg * 512:(tg + 1) * 512]:
                          e.matmul(out=o, lhsT=onesrow[:], rhs=rowbf[:], start=True, stop=True),
                          w=[("V", v3)], inc="P")
                v5 = p.o("vector", lambda e, o=mbcb[:, tg * 512:(tg + 1) * 512],
                         i=PSA[:, tg * 512:(tg + 1) * 512]: e.tensor_copy(out=o, in_=i),
                         w=[("P", pb1)], inc="V")
                v6 = p.o("vector", lambda e, o=rrowS[0:1, tg * 512:(tg + 1) * 512]:
                         e.tensor_copy(out=o, in_=row5c[:]), w=[("A", a1)], inc="V")
                v7 = p.o("vector", lambda e, o=r2rowS[0:1, tg * 512:(tg + 1) * 512]:
                         e.tensor_tensor(out=o, in0=row5c[:], in1=row5c[:], op=ALU.mult),
                         w=[("V", v6)], inc="V")
                ptr = None
                for j in range(4):
                    tc = tg * 4 + j
                    ptr = p.o("tensor", lambda e, i=row5c[0:1, j * 128:(j + 1) * 128],
                              o=PSA[:, tg * 512: tg * 512 + 1]:
                              e.transpose(out=o, in_=i, identity=idf[0:1, 0:1]),
                              w=[("V", v5), ("A", a1)], inc="P")
                    v5 = p.o("vector", lambda e, o=rcols[:, tc:tc + 1],
                             i=PSA[:, tg * 512: tg * 512 + 1]: e.tensor_copy(out=o, in_=i),
                             w=[("P", ptr)], inc="V")
                war["rowsP"] = ptr
                war["rowsA"] = a1
            vxcb = None
            for dc in range(ECH):
                vxcb = p.o("vector", lambda e, o=xcb[:, dc * S:(dc + 1) * S]:
                           e.tensor_tensor(out=o, in0=o, in1=mbcb[:], op=ALU.subtract),
                           inc="V" if dc == ECH - 1 else None)
            return vxcb

        def new_hs(vxcb):
            return {"vxcb": vxcb,
                    "psbV": [0, 0, 0, 0], "psbA": [0, 0, 0, 0],
                    "psaV": 0, "psaA": 0,
                    "qn_done": [0, 0],
                    "qnb_rd": [0, 0],
                    "bcols_rd": [0, 0],
                    "exp1c": [0, 0, 0, 0], "vmxc": [0, 0, 0, 0], "combV": 0,
                    "nbv": 0, "lnrd": 0,
                    "sm_rd": 0, "pass1A": 0, "pass1V": 0,
                    "exp2p": [0, 0], "vao": 0, "aostore": [None, None],
                    "bbc_mm": 0, "bbc_cp": [0, 0], "vbr": 0, "vsub": [0, 0],
                    "vt_rd": 0, "smalls3": [0, 0],
                    "brow_rd": 0,
                    "pass2_end": 0}

        def issue_lv(hg):
            return p.dma("gpsimd", wv4[:].rearrange("p (m n) -> p m n", m=4 * ECH),
                         d_vblk.ap()[hg].rearrange("g j p n -> p (g j) n"),
                         w=[("P", war["wv4"])], slot="wv4")

        def plan_vt(s, hg, hs):
            lv = hs.get("lv") or issue_lv(hg)
            hs["lv"] = None
            for tc in range(16):
                tp = tc % 2
                psvt = PSB[:, tp * 512:(tp + 1) * 512]
                pv = None
                for dc in range(ECH):
                    pv = p.o("tensor", lambda e, o=psvt,
                             l=xcb[:, dc * S + tc * 128: dc * S + tc * 128 + 128],
                             r=wv4[:, dc * 512:(dc + 1) * 512],
                             st=(dc == 0), sp=(dc == ECH - 1):
                             e.matmul(out=o, lhsT=l, rhs=r, start=st, stop=sp),
                             w=[lv, ("V", hs["vxcb"]), ("V", hs["psbV"][tp]),
                                ("A", hs["psbA"][tp]), ("P", hs["pass2_end"] if tc < 2 else 0)]
                             if dc == 0 else (),
                             inc="P" if dc == ECH - 1 else None)
                ve = p.o("vector", lambda e, o=vt4[:, tc * 512:(tc + 1) * 512], i=psvt,
                         sc=rcols[:, tc:tc + 1]:
                         e.tensor_scalar(out=o, in0=i, scalar1=sc, scalar2=None,
                                         op0=ALU.mult),
                         w=[("P", pv)], inc="V")
                hs["psbV"][tp] = ve
            war["wv4"] = p.cnt["P"]

        def plan_qkv_gen(s, h, hs):
            hp = h % 2
            lq = p.dma("gpsimd",
                       wqb[:, hp * E:(hp + 1) * E].rearrange("p (g n) -> p g n", g=ECH),
                       d_qblk.ap()[h].rearrange("g p n -> p g n"),
                       w=[("P", war["wq"][hp])], slot="wq%d" % hp)
            lk = p.dma("gpsimd",
                       wkb[:, hp * E:(hp + 1) * E].rearrange("p (g n) -> p g n", g=ECH),
                       d_kblk.ap()[h].rearrange("g p n -> p g n"),
                       w=[("P", war["wk"][hp])], slot="wk%d" % hp)
            if h % 4 == 0 and h > 0:
                hs["lv"] = issue_lv(h // 4)
            qdone = 0
            for (wbuf, lw, nbuf, scl, wark) in ((wqb, lq, qn_bufs[hp], qsc[:], "wq"),
                                                (wkb, lk, kn_bufs[hp], ksc[:], "wk")):
                pq = [None] * 4
                for dc in range(ECH):
                    for tg in range(3):
                        pq[tg] = p.o("tensor", lambda e, o=PSB[:, tg * 512:(tg + 1) * 512],
                                     l=wbuf[:, hp * E + dc * 128: hp * E + (dc + 1) * 128],
                                     r=xcb[:, dc * S + tg * 512: dc * S + (tg + 1) * 512],
                                     st=(dc == 0), sp=(dc == ECH - 1):
                                     e.matmul(out=o, lhsT=l, rhs=r, start=st, stop=sp),
                                     w=[lw, ("V", hs["vxcb"]), ("P", hs["qnb_rd"][hp]),
                                        ("V", hs["psbV"][tg]), ("A", hs["psbA"][tg])]
                                     if dc == 0 else (),
                                     inc="P" if dc == ECH - 1 else None)
                    if dc % 4 == 3:
                        yield
                for tg in range(4):
                    bk = tg if tg < 3 else 0
                    if tg == 3:
                        for dc in range(ECH):
                            pq[3] = p.o("tensor", lambda e, o=PSB[:, 0:512],
                                        l=wbuf[:, hp * E + dc * 128: hp * E + (dc + 1) * 128],
                                        r=xcb[:, dc * S + 3 * 512: dc * S + 4 * 512],
                                        st=(dc == 0), sp=(dc == ECH - 1):
                                        e.matmul(out=o, lhsT=l, rhs=r, start=st, stop=sp),
                                        w=[("V", hs["psbV"][0]), ("A", hs["psbA"][0])]
                                        if dc == 0 else (),
                                        inc="P" if dc == ECH - 1 else None)
                            if dc % 8 == 7:
                                yield
                    tp = tg % 2
                    psq = PSB[:, bk * 512:(bk + 1) * 512]
                    asq = p.o("scalar", lambda e, o=sq2[:, tp * 512:(tp + 1) * 512], i=psq:
                              e.activation(out=o, in_=i, func=AF.Square),
                              w=[("P", pq[tg])],
                              inc="A")
                    hs["psbA"][bk] = asq
                    yield
                    pm2 = p.o("tensor", lambda e, r=sq2[:, tp * 512:(tp + 1) * 512]:
                              e.matmul(out=PSB[0:1, 1536:2048], lhsT=ones1[:], rhs=r,
                                       start=True, stop=True),
                              w=[("A", asq), ("V", hs["smalls3"][0]), ("A", hs["smalls3"][1])],
                              inc="P")
                    vtr = p.o("vector", lambda e, r2=r2rowS[0:1, tg * 512:(tg + 1) * 512]:
                              e.tensor_tensor(out=rrow[:], in0=PSB[0:1, 1536:2048], in1=r2,
                                              op=ALU.mult),
                              w=[("P", pm2), ("V", hs["lnrd"])], inc="V")
                    a0 = p.o("scalar", lambda e: e.activation(out=rrow[:], in_=rrow[:], func=AF.Ln,
                                                              bias=eps6[0:1, :], scale=1.0 / DH),
                             w=[("V", vtr)], inc="A")
                    a1 = p.o("scalar", lambda e: e.activation(out=rrow[:], in_=rrow[:],
                                                              func=AF.Exp, scale=-0.5),
                             w=[("A", a0)], inc="A")
                    vcr = p.o("vector", lambda e, rr=rrowS[0:1, tg * 512:(tg + 1) * 512]:
                              e.tensor_tensor(out=rowbf[:], in0=rrow[:], in1=rr, op=ALU.mult),
                              w=[("A", a1)], inc="V")
                    hs["lnrd"] = vcr
                    yield
                    pbc = p.o("tensor", lambda e:
                              e.matmul(out=PSB[:, 1536:2048], lhsT=onesrow[:], rhs=rowbf[:],
                                       start=True, stop=True),
                              w=[("V", vcr)], inc="P")
                    vcb = p.o("vector", lambda e, o=cbc[:, tp * 512:(tp + 1) * 512]:
                              e.tensor_copy(out=o, in_=PSB[:, 1536:2048]),
                              w=[("P", pbc)], inc="V")
                    hs["smalls3"] = [vcb, a1]
                    vqn = p.o("vector", lambda e, o=nbuf[:, tg * 512:(tg + 1) * 512],
                              i=psq, sc=scl, b=cbc[:, tp * 512:(tp + 1) * 512]:
                              e.scalar_tensor_tensor(out=o, in0=i, scalar=sc, in1=b,
                                                     op0=ALU.mult, op1=ALU.mult),
                              w=[("V", vcb)], inc="V")
                    hs["psbV"][bk] = vqn
                    qdone = vqn
                    yield
                war[wark][hp] = pq[3]
            hs["qn_done"][hp] = qdone

        def plan_pass1_qt(s, h, qt, hs):
            hp = h % 2
            qn = qn_bufs[hp]
            kn = kn_bufs[hp]
            for c in range(4):
                mxcol = t_mx4[:, hp * 64 + qt * 4 + c: hp * 64 + qt * 4 + c + 1]
                smacc = t_sm[:, (qt * 4 + c) * 8: (qt * 4 + c) * 8 + 1]
                ps1 = p.o("tensor", lambda e, o=PSA[:, c * 512:(c + 1) * 512],
                          l=qn[:, qt * 128:(qt + 1) * 128], r=kn[:, c * 512:(c + 1) * 512]:
                          e.matmul(out=o, lhsT=l, rhs=r, start=True, stop=True),
                          w=[("V", hs["qn_done"][hp]), ("A", hs["exp1c"][c]),
                             ("V", hs["vmxc"][c]),
                             ("A", hs["psaA"]) if c == 0 else None,
                             ("V", hs["psaV"]) if c == 0 else None],
                          inc="P")
                vmxc = p.o("vector", lambda e, o=mxcol, i=PSA[:, c * 512:(c + 1) * 512]:
                           e.reduce_max(out=o, in_=i, axis=AX.X, negate=True),
                           w=[("P", ps1), ("A", hs["exp1c"][c]),
                              ("V", hs["combV"]) if qt == 0 else None],
                           inc="V")
                hs["vmxc"][c] = vmxc
                ax = p.o("scalar", lambda e, o=ptmp[:, c * 512:(c + 1) * 512],
                         i=PSA[:, c * 512:(c + 1) * 512], b=mxcol, a=smacc:
                         e.activation(out=o, in_=i, func=AF.Exp, bias=b, scale=1.0,
                                      accum_out=a),
                         w=[("V", vmxc), ("V", hs["combV"]) if qt == 0 and c == 0 else None],
                         inc="A")
                hs["exp1c"][c] = ax

        def plan_pass1_combine(s, h, hs):
            hp = h % 2
            mx64 = t_mx4[:, hp * 64:(hp + 1) * 64]
            mx3d = mx64.rearrange("p (q c) -> p q c", q=16)
            sm3d = t_sm[:].rearrange("p (q c x) -> p q c x", q=16, c=4)[:, :, :, 0]
            vmx = p.o("vector", lambda e: e.tensor_reduce(out=t_negmx[:], in_=mx3d, axis=AX.X,
                                                          op=ALU.min),
                      w=[("V", hs["nbv"]), ("A", hs["exp1c"][3])], inc="V")
            vdf = p.o("vector", lambda e: e.tensor_tensor(
                out=mx3d, in0=mx3d,
                in1=t_negmx[:].unsqueeze(2).broadcast_to([128, 16, 4]), op=ALU.subtract),
                      w=[("V", vmx)], inc="V")
            adf = p.o("scalar", lambda e: e.activation(out=mx64, in_=mx64, func=AF.Exp,
                                                       scale=-1.0),
                      w=[("V", vdf)], inc="A")
            vdt = p.o("vector", lambda e: e.tensor_tensor(out=mx3d, in0=mx3d, in1=sm3d,
                                                          op=ALU.mult),
                      w=[("A", adf)], inc="V")
            vsm = p.o("vector", lambda e: e.tensor_reduce(out=t_smr[:], in_=mx3d, axis=AX.X,
                                                          op=ALU.add),
                      w=[("V", vdt)], inc="V")
            aln = p.o("scalar", lambda e: e.activation(out=t_smr[:], in_=t_smr[:], func=AF.Ln),
                      w=[("V", vsm)], inc="A")
            vnb = p.o("vector", lambda e: e.tensor_tensor(out=t_negmx[:], in0=t_negmx[:],
                                                          in1=t_smr[:], op=ALU.subtract),
                      w=[("A", aln)], inc="V")
            bhl3 = bhl[:, hp * 32:(hp + 1) * 32].rearrange("p (q t) -> p q t", q=16)
            vnb2 = p.o("vector", lambda e: e.tensor_copy(out=nbhi16[:], in_=t_negmx[:]),
                       w=[("V", vnb)], inc="V")
            vnb3 = p.o("vector", lambda e, o=bhl3[:, :, 0]: e.tensor_copy(out=o, in_=nbhi16[:]),
                       w=[("V", vnb2), ("P", hs["bcols_rd"][hp])], inc="V")
            nbv = p.o("vector", lambda e, o=bhl3[:, :, 1]:
                      e.tensor_tensor(out=o, in0=t_negmx[:], in1=nbhi16[:], op=ALU.subtract),
                      w=[("V", vnb3)], inc="V")
            hs["nbv"] = nbv
            hs["combV"] = nbv

        def plan_pass2(s, h, hs):
            hp = h % 2
            qn = qn_bufs[hp]
            kn = kn_bufs[hp]
            j = h % 4

            def bbuild_steps(qg, first):
                steps = []
                for jj in range(4):
                    qt = qg * 4 + jj
                    def t_step(jj=jj, qt=qt, first=first):
                        ptr = p.o("tensor",
                                  lambda e,
                                  i=bhl[:, hp * 32 + qt * 2: hp * 32 + qt * 2 + 2],
                                  o=PSA[0:2, 1536:1664]:
                                  e.transpose(out=o, in_=i, identity=idf[:]),
                                  w=[("V", hs["nbv"]) if first and jj == 0 else None,
                                     ("A", hs["pass1A"]) if first and jj == 0 else None,
                                     ("V", hs["pass1V"]) if first and jj == 0 else None,
                                     ("V", hs["vbr"]) if True else None],
                                  inc="P")
                        hs["vbr"] = p.o("vector",
                                        lambda e, o=frhs[:, qt * 128:(qt + 1) * 128],
                                        i=PSA[0:2, 1536:1664]: e.tensor_copy(out=o, in_=i),
                                        w=[("P", ptr)], inc="V")
                    steps.append(t_step)
                return steps

            pending = bbuild_steps(0, True)
            for qg in range(4):
                gp = qg % 2
                for st_ in pending:
                    st_()
                pending = bbuild_steps(qg + 1, False) if qg + 1 < 4 else []
                vfr = hs["vbr"]
                prev = None
                pao = None
                for kc in range(16):
                    if kc >= 6 and pending:
                        pending.pop(0)()
                    pp = kc % 2
                    psT = PSA[:, pp * 512:(pp + 1) * 512]
                    p.o("tensor", lambda e, o=psT, l=kn[:, kc * 128:(kc + 1) * 128],
                        r=qn[:, qg * 512:(qg + 1) * 512]:
                        e.matmul(out=o, lhsT=l, rhs=r, start=True, stop=False),
                        w=[("A", hs["exp2p"][pp]),
                           ("A", hs["pass1A"]) if kc < 2 and qg == 0 else None,
                           ("V", hs["pass1V"]) if kc < 2 and qg == 0 else None])
                    p2 = p.o("tensor", lambda e, o=psT, r=frhs[:, qg * 512:(qg + 1) * 512]:
                             e.matmul(out=o, lhsT=ones2[:], rhs=r, start=False, stop=True),
                             w=[("V", vfr) if kc == 0 else None],
                             inc="P")
                    ax2 = p.o("scalar", lambda e, o=ptile[:, pp * 512:(pp + 1) * 512], i=psT:
                              e.activation(out=o, in_=i, func=AF.Exp),
                              w=[("P", p2)], inc="A")
                    hs["exp2p"][pp] = ax2
                    if prev is not None:
                        pkc = prev[1]
                        p.o("tensor", lambda e, o=PSA[:, 1024:1536],
                            l=vt4[:, pkc * 512 + j * 128: pkc * 512 + (j + 1) * 128],
                            r=ptile[:, (pkc % 2) * 512:((pkc % 2) + 1) * 512],
                            st=(pkc == 0):
                            e.matmul(out=o, lhsT=l, rhs=r, start=st, stop=False),
                            w=[("A", prev[0]),
                               ("V", hs["vao"]) if pkc == 0 else None])
                    prev = (ax2, kc)
                pao = p.o("tensor", lambda e, o=PSA[:, 1024:1536],
                          l=vt4[:, 15 * 512 + j * 128: 15 * 512 + (j + 1) * 128],
                          r=ptile[:, 512:1024]:
                          e.matmul(out=o, lhsT=l, rhs=r, start=False, stop=True),
                          w=[("A", prev[0])], inc="P")
                vao = p.o("vector", lambda e, o=aost[:, gp * 512:(gp + 1) * 512],
                          i=PSA[:, 1024:1536]:
                          e.tensor_copy(out=o, in_=i),
                          w=[("P", pao), hs["aostore"][gp]], inc="V")
                hs["vao"] = vao
                hs["aostore"][gp] = p.dma(
                    "gpsimd", d_aosc.ap()[h][:, qg * 512:(qg + 1) * 512],
                    aost[:, gp * 512:(gp + 1) * 512], w=[("V", vao)], slot="aost%d" % gp)
            hs["qnb_rd"][hp] = p.cnt["P"]
            hs["bcols_rd"][hp] = p.cnt["P"]
            hs["pass2_end"] = p.cnt["P"]
            hs["psaV"] = p.cnt["V"]
            hs["psaA"] = p.cnt["A"]

        def plan_wo(s):
            for qtb in range(4):
                for eg in range(4):
                    gp = (qtb * 4 + eg) % 2
                    banks = PS[gp]
                    lastp = None
                    for hh in range(H):
                        hp2 = hh % 4
                        la = p.dma("sync", aobuf[:, hp2 * 512:(hp2 + 1) * 512],
                                   d_aosc.ap()[hh][:, qtb * 512:(qtb + 1) * 512],
                                   w=[("P", war["aobuf"][hp2])] +
                                     (war["ao_all"] if (hh == 0 and qtb == 0 and eg == 0)
                                      else []), slot="aob%d" % hp2)
                        lo = p.dma("sync", wosb[:, hp2 * 512:(hp2 + 1) * 512],
                                   d_osc.ap()[hh][:, eg * 512:(eg + 1) * 512],
                                   w=[("P", war["wosb"][hp2])], slot="wos%d" % hp2)
                        for ee in range(4):
                            lastp = p.o("tensor", lambda e, o=banks[:, ee * 512:(ee + 1) * 512],
                                        l=wosb[:, hp2 * 512 + ee * 128: hp2 * 512 + (ee + 1) * 128],
                                        r=aobuf[:, hp2 * 512:(hp2 + 1) * 512],
                                        st=(hh == 0), sp=(hh == H - 1):
                                        e.matmul(out=o, lhsT=l, rhs=r, start=st, stop=sp),
                                        w=[lo, la,
                                           ("V", war["bank_read"][gp]) if (hh == 0 and ee == 0)
                                           else None,
                                           ("A", war["bank_readA"]) if (hh == 0 and ee == 0)
                                           else None] if (hh == 0 or ee == 0) else (),
                                        inc="P" if ee == 3 else None)
                        war["aobuf"][hp2] = lastp
                        war["wosb"][hp2] = lastp
                    pwv = lastp
                    for ee in range(4):
                        ec = eg * 4 + ee
                        ep2 = ee % 2
                        xev = xepb[:, ep2 * 512:(ep2 + 1) * 512]
                        lx = p.dma("sync", xev,
                                   src.ap()[s, ec * 128:(ec + 1) * 128, qtb * 512:(qtb + 1) * 512],
                                   w=[("V", war["xepb"][ep2])], slot="xep%d" % ep2)
                        vy = p.o("vector", lambda e, o=ystb[:, ep2 * 512:(ep2 + 1) * 512],
                                 i=banks[:, ee * 512:(ee + 1) * 512], x=xev:
                                 e.tensor_tensor(out=o, in0=i, in1=x, op=ALU.add),
                                 w=[("P", pwv), lx, war["ystb"][ep2]], inc="V")
                        war["xepb"][ep2] = vy
                        war["ystb"][ep2] = p.dma(
                            "gpsimd", dst.ap()[s, ec * 128:(ec + 1) * 128,
                                               qtb * 512:(qtb + 1) * 512],
                            ystb[:, ep2 * 512:(ep2 + 1) * 512], w=[("V", vy)],
                            slot="yst%d" % ep2)
                    war["bank_read"][gp] = p.cnt["V"]

        for s in range(2):
            if s == 0:
                plan_osc_build()
            vxcb = plan_loadx(s)
            hs = new_hs(vxcb)
            plan_vt(s, 0, hs)
            for _ in plan_qkv_gen(s, 0, hs):
                pass
            for h in range(H):
                gen = plan_qkv_gen(s, h + 1, hs) if h + 1 < H else iter(())
                for qt in range(16):
                    plan_pass1_qt(s, h, qt, hs)
                    next(gen, None)
                    next(gen, None)
                for _ in gen:
                    pass
                plan_pass1_combine(s, h, hs)
                hs["pass1A"] = hs["exp1c"][3]
                hs["pass1V"] = hs["nbv"]
                plan_pass2(s, h, hs)
                if h + 1 < H and (h + 1) % 4 == 0:
                    plan_vt(s, (h + 1) // 4, hs)
            war["ao_all"] = [(sname, p.cnt[sname]) for sname in ("D_aost0", "D_aost1")
                             if p.cnt.get(sname)]
            war["bank_read"] = [p.cnt["V"], p.cnt["V"]]
            war["bank_readA"] = p.cnt["A"]
            plan_wo(s)
            p.barrier()
        es.close()

    only = lambda ph: phases[-1] == ph
    if "A" in phases:
        plan_ffn(d_xT, d_yT if only("A") else d_x1, build_scratch=True, tag="A")
    if "B" in phases:
        plan_attn(d_x1 if "A" in phases else d_xT, d_yT if only("B") else d_x2)
    if "C" in phases:
        plan_ffn(d_x2 if "B" in phases else d_xT, d_yT,
                 build_scratch=("A" not in phases), tag="C")
    allw = [(sname, v) for sname, v in p.cnt.items() if sname.startswith("D_") and v > 0]
    fin_s, fin_v = allw[-1]
    p.o("sync", lambda e, s=fin_s, v=fin_v: e.wait_ge(p.sems[s], v), w=allw[:-1])
    p.emit()
    es_glob.close()
    p.es.close()
    return nc


def _prep_shared(wq_gate, wq_up, wq_down, wq, wk, wv, wo,
                 s_gate, s_up, s_down, q_scale, k_scale):
    c = np.ascontiguousarray
    gblk = c(wq_gate.T.reshape(ECH, 128, HCH, 128).transpose(2, 0, 1, 3))
    ublk = c(wq_up.T.reshape(ECH, 128, HCH, 128).transpose(2, 0, 1, 3))
    dblk = c(wq_down.T.reshape(HCH, 128, ECH, 128).transpose(2, 0, 1, 3))
    qblk = c(wq.reshape(E, E).reshape(ECH, 128, H, 128).transpose(2, 0, 1, 3))
    kblk = c(wk.reshape(E, E).reshape(ECH, 128, H, 128).transpose(2, 0, 1, 3))
    vblk = c(wv.reshape(E, E).reshape(ECH, 128, H, 128).transpose(2, 0, 1, 3)
             .reshape(H // 4, 4, ECH, 128, 128).transpose(0, 2, 1, 3, 4))
    oblk = c(wo.reshape(E, E).reshape(H, 128, ECH, 128).transpose(0, 2, 1, 3))
    sg = c(s_gate.reshape(HCH, 128).T)
    su = c(s_up.reshape(HCH, 128).T)
    sd = c(s_down.reshape(ECH, 128).T)
    qs = c(q_scale.reshape(128, 1))
    ks = c(k_scale.reshape(128, 1))
    return dict(gblk=gblk, ublk=ublk, dblk=dblk, qblk=qblk, kblk=kblk,
                vblk=vblk, oblk=oblk, sg=sg, su=su, sd=sd, qs=qs, ks=ks)


def kernel(L0_attn_x, L0_attn_wq, L0_attn_wk, L0_attn_wv, L0_attn_wo,
           L0_attn_q_scale, L0_attn_k_scale,
           L0_ffn_wq_gate, L0_ffn_s_gate, L0_ffn_wq_up, L0_ffn_s_up,
           L0_ffn_wq_down, L0_ffn_s_down):
    x = np.asarray(L0_attn_x, dtype=np.float32)
    shared = _prep_shared(np.asarray(L0_ffn_wq_gate), np.asarray(L0_ffn_wq_up),
                          np.asarray(L0_ffn_wq_down),
                          np.asarray(L0_attn_wq, dtype=np.float32),
                          np.asarray(L0_attn_wk, dtype=np.float32),
                          np.asarray(L0_attn_wv, dtype=np.float32),
                          np.asarray(L0_attn_wo, dtype=np.float32),
                          np.asarray(L0_ffn_s_gate, dtype=np.float32),
                          np.asarray(L0_ffn_s_up, dtype=np.float32),
                          np.asarray(L0_ffn_s_down, dtype=np.float32),
                          np.asarray(L0_attn_q_scale, dtype=np.float32),
                          np.asarray(L0_attn_k_scale, dtype=np.float32))
    nc = build_program()
    in_maps = []
    for cid in range(NCORES):
        xT = np.ascontiguousarray(x[2 * cid: 2 * cid + 2].transpose(0, 2, 1))
        m = dict(xT=xT)
        m.update(shared)
        in_maps.append(m)
    res = run_bass_kernel_spmd(nc, in_maps, core_ids=list(range(NCORES)))
    y = np.empty((16, S, E), dtype=np.float32)
    for cid in range(NCORES):
        y[2 * cid: 2 * cid + 2] = res.results[cid]["yT"].transpose(0, 2, 1)
    return y
